# version 16
# speedup vs baseline: 3.0351x; 1.0044x over previous
import numpy as np
import ml_dtypes
from contextlib import ExitStack

import concourse.bass as bass
import concourse.mybir as mybir
from concourse.bass_utils import run_bass_kernel_spmd

F32 = mybir.dt.float32
BF16 = mybir.dt.bfloat16
AF = mybir.ActivationFunctionType
ALU = mybir.AluOpType
AX = mybir.AxisListType
AP = bass.AP

T, E, NH, HD = 4096, 1024, 16, 64
NP = NH // 2
QTW, NQT = 512, T // 512
NSUB = T // 128
FF, NE = 4096, 8
TCH, NCH = 256, T // 256
EPS = 1e-5
EXPB = -12.0


class Sem:
    __slots__ = ("h", "n")

    def __init__(self, h):
        self.h = h
        self.n = 0


class C:

    def __init__(self, role, eng, sems):
        self.role = role
        self.eng = eng
        self.sems = sems

    def inc(self, sem, instr, k=1):
        sem.n += k
        if instr is not None:
            instr.then_inc(sem.h, k)

    def wait(self, sem, n=None):
        v = sem.n if n is None else n
        if v > 0:
            self.eng.wait_ge(sem.h, v)


def build(debug=False):
    nc = bass.Bass("TRN2", target_bir_lowering=False)
    dbgkind = "ExternalOutput" if debug else "Internal"

    x_d = nc.dram_tensor("x", [T, E], F32, kind="ExternalInput").ap()
    wkqv_d = nc.dram_tensor("wkqv", [E, 3 * E], F32, kind="ExternalInput").ap()
    wproj_d = nc.dram_tensor("wproj", [E, E], F32, kind="ExternalInput").ap()
    bproj_d = nc.dram_tensor("bproj", [E], F32, kind="ExternalInput").ap()
    gatew_d = nc.dram_tensor("gatew", [E, NE], F32, kind="ExternalInput").ap()
    w1_d = nc.dram_tensor("w1", [NE, E, FF], F32, kind="ExternalInput").ap()
    b1_d = nc.dram_tensor("b1", [NE, FF], F32, kind="ExternalInput").ap()
    w2_d = nc.dram_tensor("w2", [NE, FF, E], F32, kind="ExternalInput").ap()
    b2_d = nc.dram_tensor("b2", [NE, E], F32, kind="ExternalInput").ap()
    out_d = nc.dram_tensor("out", [T, E], F32, kind="ExternalOutput").ap()
    dbg_d = nc.dram_tensor("dbg", [1, 80], F32, kind="ExternalOutput").ap() if debug else None
    if debug:
        dbg_kt = nc.dram_tensor("dbg_kt", [128, NP * T], BF16, kind="ExternalOutput").ap()
        dbg_v = nc.dram_tensor("dbg_v", [128, NSUB * E], BF16, kind="ExternalOutput").ap()
        dbg_qt = nc.dram_tensor("dbg_qt", [128, NP * QTW], BF16, kind="ExternalOutput").ap()
        dbg_att = nc.dram_tensor("dbg_att", [128, NP * QTW], BF16, kind="ExternalOutput").ap()

    wkqv_scr = nc.dram_tensor("wkqv_scr", [E, 3 * E], BF16, kind="Internal").ap()
    x1_scr = nc.dram_tensor("x1_scr", [T, E], F32, kind=dbgkind).ap()
    xn1t_scr = nc.dram_tensor("xn1t_scr", [E, T], BF16, kind=dbgkind).ap()
    part_scr = [nc.dram_tensor(f"part{i}", [T, E], F32, kind="Internal").ap() for i in range(2)]
    vec_scr = nc.dram_tensor("vec_scr", [E], F32, kind="Internal").ap()

    ident_d = nc.inline_tensor(np.asarray(np.eye(128, dtype=ml_dtypes.bfloat16)), name="identc").ap()
    ones_d = nc.inline_tensor(np.ones((128, 128), dtype=np.float32), name="onesc").ap()
    onesb_d = nc.inline_tensor(np.asarray(np.ones((128, 2), dtype=ml_dtypes.bfloat16)), name="onesbc").ap()
    uu = np.arange(128)[None, :]
    pp_ = np.arange(128)[:, None]
    tri_d = nc.inline_tensor(np.asarray((pp_ <= uu).astype(ml_dtypes.bfloat16)), name="tric").ap()
    onesrow_d = nc.inline_tensor(np.asarray(np.ones((1, 128), dtype=ml_dtypes.bfloat16)), name="onesrowc").ap()
    iota8_d = nc.inline_tensor(np.arange(8, dtype=np.float32)[None, :], name="iota8c").ap()

    sems = {}
    ctxstack = ExitStack()

    def mksem(name):
        s = Sem(ctxstack.enter_context(nc.semaphore(name)))
        sems[name] = s
        return s

    S = type("S", (), {})()
    for nm in ["p0_ld", "p0_t", "p0_v", "p0_st",
               "a_x", "a_slab", "a_slab2", "a_tslab", "a_lnv", "a_lns", "a_xn",
               "a_tr", "a_trev", "a_kqv", "a_kqst", "a_vev",
               "b_s", "b_exp", "b_msk", "b_pv", "b_nrm", "b_rmm",
               "c_xr", "c_prj", "c_x1", "c_st",
               "d_x", "d_lnv", "d_lns", "d_xnf", "d_xnb", "d_tr", "d_trev", "d_sum", "d_st",
               "e_ld", "e_t", "e_s", "e_v",
               "f_w", "f_xn", "f_res", "f_h1", "f_gel", "f_y", "f_ev", "f_st"]:
        setattr(S, nm, mksem(nm))

    glob = ExitStack()

    def sb(stack, name, shape, dt):
        return stack.enter_context(nc.sbuf_tensor(name, shape, dt))

    ident_sb = sb(glob, "ident_sb", [128, 128], BF16)
    ones_sb = sb(glob, "ones_sb", [128, 128], F32)
    onesb_sb = sb(glob, "onesb_sb", [128, 2], BF16)
    ln_sb = sb(glob, "ln_sb", [128, 16], F32)
    cst_sb = sb(glob, "cst_sb", [128, 2], F32)
    ps_main = glob.enter_context(nc.psum_tensor("ps_main", [128, 3584], F32))
    ps_tr = glob.enter_context(nc.psum_tensor("ps_tr", [128, 1024], BF16))

    def run_phase(fn):
        saved = {k: s.n for k, s in sems.items()}
        with nc.Block() as blk:
            for role, starter in (("sync", blk.sync), ("gpsimd", blk.gpsimd),
                                  ("tensor", blk.tensor), ("vector", blk.vector),
                                  ("scalar", blk.scalar)):
                for k, s in sems.items():
                    s.n = saved[k]

                def body(eng, role=role):
                    fn(C(role, eng, sems))

                starter(body)

    def ln_stats(c, src_ap, sqdump_ap, s_v, s_s):
        me, eng = c.role, c.eng
        if me == "vector":
            c.inc(s_v, eng.reduce_sum(ln_sb[:, 0:1], src_ap, axis=AX.X, negate=True))
            c.inc(s_v, eng.scalar_tensor_tensor(sqdump_ap, src_ap, 1.0, src_ap,
                                                ALU.mult, ALU.mult, accum_out=ln_sb[:, 6:7]))
        else:
            c.inc(s_v, None)
            c.inc(s_v, None)
        if me == "scalar":
            c.wait(s_v)
            c.inc(s_s, eng.activation(ln_sb[:, 1:2], ln_sb[:, 0:1], AF.Copy, scale=1.0 / E))
            c.wait(s_s)
            c.inc(s_s, eng.activation(ln_sb[:, 2:3], ln_sb[:, 1:2], AF.Square))
        else:
            c.inc(s_s, None)
            c.inc(s_s, None)
        if me == "vector":
            c.wait(s_s)
            c.inc(s_v, eng.tensor_scalar(ln_sb[:, 3:4], ln_sb[:, 6:7], 1.0 / E, ln_sb[:, 2:3],
                                         ALU.mult, ALU.subtract))
        else:
            c.inc(s_v, None)
        if me == "scalar":
            c.wait(s_v)
            c.inc(s_s, eng.activation(ln_sb[:, 4:5], ln_sb[:, 3:4], AF.Ln, bias=cst_sb[:, 0:1]))
            c.wait(s_s)
            c.inc(s_s, eng.activation(ln_sb[:, 5:6], ln_sb[:, 4:5], AF.Exp, scale=-0.5))
        else:
            c.inc(s_s, None)
            c.inc(s_s, None)

    attn = ExitStack()
    kt_sb = sb(attn, "kt_sb", [128, NP * T], BF16)
    v_sb = sb(attn, "v_sb", [128, NSUB * E], BF16)
    qt_sb = sb(attn, "qt_sb", [128, NP * QTW], BF16)
    slab_sb = sb(attn, "slab_sb", [128, 3 * 3072], BF16)
    wproj_sb = sb(attn, "wproj_sb", [128, 8 * E], BF16)
    tri_sb = sb(attn, "tri_sb", [128, 128], BF16)
    onesrow_sb = sb(attn, "onesrow_sb", [1, 128], BF16)
    bprojt_sb = sb(attn, "bprojt_sb", [1, E], BF16)
    xln_sb = sb(attn, "xln_sb", [128, 2 * E], F32)
    xn_sb = sb(attn, "xn_sb", [128, E], BF16)
    xnt_sb = sb(attn, "xnt_sb", [128, E], BF16)
    kq_sb = sb(attn, "kq_sb", [128, E], BF16)
    pt_sb = sb(attn, "pt_sb", [128, 2048], BF16)
    att_sb = sb(attn, "att_sb", [128, NP * QTW], BF16)
    rr_sb = sb(attn, "rr_sb", [128, 1024], F32)
    x1_sb = sb(attn, "x1_sb", [128, E], F32)

    def p0(c):
        me, eng = c.role, c.eng
        if me == "sync":
            for dst, srcd in ((ident_sb[:, :], ident_d), (ones_sb[:, :], ones_d),
                              (onesb_sb[:, :], onesb_d), (tri_sb[:, :], tri_d),
                              (onesrow_sb[:, :], onesrow_d)):
                c.inc(S.p0_ld, eng.dma_start(dst, srcd), 16)
        else:
            c.inc(S.p0_ld, None, 16 * 5)
        if me == "gpsimd":
            src = AP(wproj_d.tensor, 0, [[E, 128], [128 * E, 8], [1, E]])
            c.inc(S.p0_ld, eng.dma_start(wproj_sb[:, :], src), 16)
            src = AP(wkqv_d.tensor, 0, [[3 * E, 128], [128 * 3 * E, 8], [1, 3 * E]])
            c.inc(S.p0_ld, eng.dma_start(kt_sb[:, 0:8 * 3072], src), 16)
            c.inc(S.p0_ld, eng.dma_start(bprojt_sb[:, :], AP(bproj_d.tensor, 0, [[E, 1], [1, E]])), 16)
        else:
            c.inc(S.p0_ld, None, 48)
        if me == "sync":
            c.wait(S.p0_ld)
            dst = AP(wkqv_scr.tensor, 0, [[3 * E, 128], [128 * 3 * E, 8], [1, 3 * E]])
            c.inc(S.p0_st, eng.dma_start(dst, kt_sb[:, 0:8 * 3072]), 16)
        else:
            c.inc(S.p0_st, None, 16)
        if me == "vector":
            eng.memset(cst_sb[:, 0:1], EPS)
            i = eng.memset(cst_sb[:, 1:2], EXPB)
            c.inc(S.p0_v, i)
        else:
            c.inc(S.p0_v, None)
        if me == "sync":
            c.wait(S.p0_st)

    run_phase(p0)

    def p1(c):
        me, eng = c.role, c.eng
        st = type("st", (), {})()
        st.xn_free = [0, 0]
        st.stageC_free = 0
        st.trev_count = {}
        st.vev_prev = 0
        st.kqst_prev = 0
        st.kst_cnt = 0
        st.ktr_cnt = 0
        st.xntr = {}
        st.kqtr_prev = 0
        st.pstr_free = 0
        st.expfree = [0, 0]
        st.ptfree = [0, 0]
        st.scount = [0] * 64
        st.nrmfree = [0, 0]
        st.rsfree = [0, 0]
        st.revac_done = 0
        st.prj_all_prev = 0
        st.x1st_free = 0
        st.x1ev_free = [0, 0]
        st.pending_R = None

        def prep_io(g):
            buf = g % 2
            if me == "scalar":
                c.wait(S.a_xn, st.xn_free[buf])
                if g % 4 < 2:
                    c.wait(S.c_x1, st.x1ev_free[buf] if g % 4 == 0 else st.stageC_free)
                i = eng.dma_start(xln_sb[:, buf * E:(buf + 1) * E],
                                  x_d[g * 128:(g + 1) * 128, :])
                c.inc(S.a_x, i, 16)
            else:
                c.inc(S.a_x, None, 16)
            for k in range(8):
                role = "sync" if k % 2 == 0 else "gpsimd"
                if me == role:
                    w = 8 * g + k - 2 if k >= 3 else [8 * g - 1, 8 * g, 8 * g - 2][k]
                    c.wait(S.a_tslab, max(0, w))
                    if g == 0 and k < 3:
                        c.wait(S.p0_st, 16)
                    i = eng.dma_start(slab_sb[:, (k % 3) * 3072:(k % 3 + 1) * 3072],
                                      wkqv_scr[k * 128:(k + 1) * 128, :])
                    c.inc(S.a_slab if k % 2 == 0 else S.a_slab2, i, 16)
                else:
                    c.inc(S.a_slab if k % 2 == 0 else S.a_slab2, None, 16)

        def prep_ln(g):
            buf = g % 2
            lb = (g % 2) * 8
            if me == "vector":
                c.wait(S.a_x, 16 * (g + 1))
                if g % 4 == 0:
                    c.wait(S.b_pv)
                xsrc = xln_sb[:, buf * E:(buf + 1) * E]
                i = eng.reduce_sum(ln_sb[:, lb:lb + 1], xsrc, axis=AX.X, negate=True)
                c.inc(S.a_lnv, i)
                i = eng.scalar_tensor_tensor(pt_sb[:, 0:1024], xsrc, 1.0, xsrc,
                                             ALU.mult, ALU.mult, accum_out=ln_sb[:, lb + 6:lb + 7])
                c.inc(S.a_lnv, i)
                c.wait(S.a_lnv)
                i = eng.tensor_scalar(ln_sb[:, lb + 1:lb + 2], ln_sb[:, lb:lb + 1],
                                      1.0 / E, None, ALU.mult)
                c.inc(S.a_lnv, i)
                c.wait(S.a_lnv)
                i = eng.tensor_tensor(ln_sb[:, lb + 2:lb + 3], ln_sb[:, lb + 1:lb + 2],
                                      ln_sb[:, lb + 1:lb + 2], ALU.mult)
                c.inc(S.a_lnv, i)
                c.wait(S.a_lnv)
                i = eng.tensor_scalar(ln_sb[:, lb + 3:lb + 4], ln_sb[:, lb + 6:lb + 7],
                                      1.0 / E, ln_sb[:, lb + 2:lb + 3], ALU.mult, ALU.subtract)
                c.inc(S.a_lnv, i)
            else:
                c.inc(S.a_lnv, None, 5)
            if me == "scalar":
                c.wait(S.a_lnv)
                i = eng.activation(ln_sb[:, lb + 4:lb + 5], ln_sb[:, lb + 3:lb + 4],
                                   AF.Ln, bias=cst_sb[:, 0:1])
                c.inc(S.a_lns, i)
                c.wait(S.a_lns)
                i = eng.activation(ln_sb[:, lb + 5:lb + 6], ln_sb[:, lb + 4:lb + 5],
                                   AF.Exp, scale=-0.5)
                c.inc(S.a_lns, i)
            else:
                c.inc(S.a_lns, None, 2)
            if me == "vector":
                c.wait(S.a_lns)
                c.wait(S.a_tr, st.xntr.get(g - 1, 0))
                i = eng.tensor_scalar(xn_sb[:, :], xln_sb[:, buf * E:(buf + 1) * E],
                                      ln_sb[:, lb + 1:lb + 2], ln_sb[:, lb + 5:lb + 6],
                                      ALU.add, ALU.mult)
                c.inc(S.a_xn, i)
            else:
                c.inc(S.a_xn, None)
            st.xn_free[buf] = S.a_xn.n

        def prep_tr(g):
            if me == "tensor":
                c.wait(S.a_xn, g + 1)
                c.wait(S.a_trev, st.pstr_free)
                i = None
                for ch in range(8):
                    i = eng.transpose(ps_tr[:, ch * 128:(ch + 1) * 128],
                                      xn_sb[:, ch * 128:(ch + 1) * 128], ident_sb[:, :])
                c.inc(S.a_tr, i)
            else:
                c.inc(S.a_tr, None)
            st.xntr[g] = S.a_tr.n
            if me == "vector":
                c.wait(S.a_tr, st.xntr[g])
                c.wait(S.a_tslab, 8 * g)
                c.inc(S.a_trev, eng.tensor_copy(xnt_sb[:, :], ps_tr[:, :]))
            else:
                c.inc(S.a_trev, None)
            st.trev_count[g] = S.a_trev.n
            st.pstr_free = S.a_trev.n

        def heavy(g):
            qt, s4 = divmod(g, 4)
            if me == "tensor":
                c.wait(S.a_trev, st.trev_count[g])
                c.wait(S.a_vev, st.vev_prev)
                c.wait(S.a_kqst, st.kqst_prev)
                if s4 == 0 and qt > 0:
                    c.wait(S.c_x1, st.stageC_free)
                    c.wait(S.b_exp)
                    c.wait(S.b_nrm)
            for k in range(8):
                if me == "tensor":
                    if k % 2 == 0:
                        c.wait(S.a_slab, 16 * (4 * g + k // 2 + 1))
                    else:
                        c.wait(S.a_slab2, 16 * (4 * g + (k + 1) // 2))
                    i = None
                    for n in range(6):
                        i = eng.matmul(ps_main[:, n * 512:(n + 1) * 512],
                                       xnt_sb[:, k * 128:(k + 1) * 128],
                                       slab_sb[:, (k % 3) * 3072 + n * 512:(k % 3) * 3072 + (n + 1) * 512],
                                       start=(k == 0), stop=(k == 7))
                    c.inc(S.a_tslab, i)
                else:
                    c.inc(S.a_tslab, None)
            if me == "scalar":
                c.wait(S.a_tslab, 8 * (g + 1))
                c.wait(S.a_tr, st.kqtr_prev)
                i = eng.activation(kq_sb[:, :], ps_main[:, 0:1024], AF.Copy)
                c.inc(S.a_kqst, i)
            else:
                c.inc(S.a_kqst, None)
            st.kst_cnt = S.a_kqst.n
            if me == "vector":
                c.wait(S.a_tslab, 8 * (g + 1))
                i = eng.tensor_copy(v_sb[:, g * E:(g + 1) * E], ps_main[:, 2048:3072])
                c.inc(S.a_vev, i)
            else:
                c.inc(S.a_vev, None)
            st.vev_prev = S.a_vev.n
            if (g + 1) % 4 != 0 and g + 1 < NSUB:
                prep_tr(g + 1)
            for which in ("K", "Q"):
                if which == "Q":
                    if me == "scalar":
                        c.wait(S.a_tr, st.ktr_cnt)
                        i = eng.activation(kq_sb[:, :], ps_main[:, 1024:2048], AF.Copy)
                        c.inc(S.a_kqst, i)
                    else:
                        c.inc(S.a_kqst, None)
                if me == "tensor":
                    c.wait(S.a_kqst, st.kst_cnt if which == "K" else S.a_kqst.n)
                    c.wait(S.a_trev, st.pstr_free)
                    i = None
                    for ch in range(8):
                        i = eng.transpose(ps_tr[:, ch * 128:(ch + 1) * 128],
                                          kq_sb[:, ch * 128:(ch + 1) * 128], ident_sb[:, :])
                    c.inc(S.a_tr, i)
                else:
                    c.inc(S.a_tr, None)
                if which == "K":
                    st.ktr_cnt = S.a_tr.n
                if me == "vector":
                    c.wait(S.a_tr)
                    if which == "K":
                        dst = AP(kt_sb, g * 128, [[NP * T, 128], [T, 8], [1, 128]])
                    else:
                        dst = AP(qt_sb, s4 * 128, [[NP * QTW, 128], [QTW, 8], [1, 128]])
                    srcp = AP(ps_tr, 0, [[1024, 128], [128, 8], [1, 128]])
                    c.inc(S.a_trev, eng.tensor_copy(dst, srcp))
                else:
                    c.inc(S.a_trev, None)
                st.pstr_free = S.a_trev.n
            st.kqtr_prev = S.a_tr.n
            st.kqst_prev = S.a_kqst.n

        def emit_R(pp, recips_done):
            par = pp % 2
            pvoff = 2048 + par * 512
            roff = 2048 + ((pp + 1) % 2) * 512
            rslanes = (0, 32) if par == 0 else (64, 96)
            if me == "tensor":
                c.wait(S.b_nrm, recips_done)
                c.wait(S.b_nrm, st.nrmfree[(pp + 1) % 2])
                eng.matmul(ps_main[0:64, roff:roff + 512],
                           ones_sb[rslanes[0]:rslanes[0] + 1, 0:64],
                           rr_sb[rslanes[0]:rslanes[0] + 1, par * 512:par * 512 + 512],
                           start=True, stop=True, tile_position=(rslanes[0], 0))
                i = eng.matmul(ps_main[64:128, roff:roff + 512],
                               ones_sb[rslanes[1]:rslanes[1] + 1, 0:64],
                               rr_sb[rslanes[1]:rslanes[1] + 1, par * 512:par * 512 + 512],
                               start=True, stop=True, tile_position=(rslanes[1], 64))
                c.inc(S.b_rmm, i)
            else:
                c.inc(S.b_rmm, None)
            if me == "vector":
                c.wait(S.b_rmm)
                if pp == 0:
                    c.wait(S.c_prj, st.prj_all_prev)
                eng.tensor_copy(rr_sb[0:64, par * 512:par * 512 + 512],
                                ps_main[0:64, roff:roff + 512])
                i = eng.tensor_copy(rr_sb[64:128, par * 512:par * 512 + 512],
                                    ps_main[64:128, roff:roff + 512])
                c.inc(S.b_nrm, i)
            else:
                c.inc(S.b_nrm, None)
            st.revac_done = S.b_nrm.n
            if me == "vector":
                eng.tensor_tensor(att_sb[0:64, pp * 512:(pp + 1) * 512],
                                  ps_main[0:64, pvoff:pvoff + 512],
                                  rr_sb[0:64, par * 512:par * 512 + 512], ALU.mult)
                i = eng.tensor_tensor(att_sb[64:128, pp * 512:(pp + 1) * 512],
                                      ps_main[64:128, pvoff:pvoff + 512],
                                      rr_sb[64:128, par * 512:par * 512 + 512], ALU.mult)
                c.inc(S.b_nrm, i)
            else:
                c.inc(S.b_nrm, None)
            st.nrmfree[par] = S.b_nrm.n

        def stage_b(qt):
            jlast = 4 * qt + 3
            for pp in range(NP):
                par = pp % 2
                pvoff = 2048 + par * 512
                rslanes = (0, 32) if par == 0 else (64, 96)
                ecount = {}
                mcount = {}
                pair_pv_base = S.b_pv.n

                def emit_S(j):
                    sbi = j % 2
                    soff = sbi * 1024
                    if me == "tensor":
                        if pp == 0 and j == 0:
                            c.wait(S.a_vev)
                            c.wait(S.a_trev)
                        c.wait(S.b_exp, st.expfree[sbi])
                        i = None
                        for half, base in ((0, 0), (1, 64)):
                            i = eng.matmul(ps_main[:, soff + half * 512: soff + (half + 1) * 512],
                                           kt_sb[base:base + 64, pp * T + j * 128: pp * T + (j + 1) * 128],
                                           qt_sb[base:base + 64, pp * QTW:(pp + 1) * QTW],
                                           start=True, stop=True)
                        c.inc(S.b_s, i)
                    else:
                        c.inc(S.b_s, None)
                    st.scount[j % 64] = S.b_s.n
                    if me == "scalar":
                        c.wait(S.b_s, st.scount[j % 64])
                        c.wait(S.b_pv, pair_pv_base + j - 1)
                        i = eng.activation(pt_sb[:, sbi * 1024:(sbi + 1) * 1024],
                                           ps_main[:, soff:soff + 1024], AF.Exp,
                                           bias=cst_sb[:, 1:2], scale=0.125)
                        c.inc(S.b_exp, i)
                    else:
                        c.inc(S.b_exp, None)
                    st.expfree[sbi] = S.b_exp.n
                    ecount[j] = S.b_exp.n
                    if j >= 4 * qt:
                        cz = (j - 4 * qt) * 128
                        if me == "vector":
                            c.wait(S.b_exp, ecount[j])
                            i = None
                            for half in range(2):
                                base = sbi * 1024 + half * 512
                                if cz > 0:
                                    eng.memset(pt_sb[:, base:base + cz], 0.0)
                                i = eng.tensor_tensor(pt_sb[:, base + cz:base + cz + 128],
                                                      pt_sb[:, base + cz:base + cz + 128],
                                                      tri_sb[:, :], ALU.mult)
                            c.inc(S.b_msk, i)
                        else:
                            c.inc(S.b_msk, None)
                        mcount[j] = S.b_msk.n
                    else:
                        mcount[j] = None

                def emit_PV(j):
                    sbi = j % 2
                    ptl = pt_sb[:, sbi * 1024:sbi * 1024 + 512]
                    pth = pt_sb[:, sbi * 1024 + 512:sbi * 1024 + 1024]
                    if me == "tensor":
                        if mcount[j] is not None:
                            c.wait(S.b_msk, mcount[j])
                        else:
                            c.wait(S.b_exp, ecount[j])
                        if j == 0:
                            c.wait(S.b_nrm, st.nrmfree[par])
                            c.wait(S.b_nrm, st.revac_done)
                            c.wait(S.b_nrm, st.rsfree[par])
                        stt = (j == 0)
                        sp = (j == jlast)
                        eng.matmul(ps_main[0:64, pvoff:pvoff + 512],
                                   v_sb[:, j * E + (2 * pp) * 64: j * E + (2 * pp) * 64 + 64],
                                   ptl, start=stt, stop=sp)
                        eng.matmul(ps_main[64:128, pvoff:pvoff + 512],
                                   v_sb[:, j * E + (2 * pp + 1) * 64: j * E + (2 * pp + 1) * 64 + 64],
                                   pth, start=stt, stop=sp, tile_position=(0, 64))
                        eng.matmul(ps_main[rslanes[0]:rslanes[0] + 1, 3072:3584], onesb_sb[:, 0:1],
                                   ptl, start=stt, stop=sp, tile_position=(0, rslanes[0]))
                        i = eng.matmul(ps_main[rslanes[1]:rslanes[1] + 1, 3072:3584], onesb_sb[:, 0:1],
                                       pth, start=stt, stop=sp, tile_position=(0, rslanes[1]))
                        c.inc(S.b_pv, i)
                    else:
                        c.inc(S.b_pv, None)
                    st.ptfree[sbi] = S.b_pv.n

                DELAY = 2
                for j in range(jlast + 1):
                    emit_S(j)
                    if j == 1 and st.pending_R is not None:
                        emit_R(*st.pending_R)
                        st.pending_R = None
                    if j - DELAY >= 0:
                        emit_PV(j - DELAY)
                if st.pending_R is not None:
                    emit_R(*st.pending_R)
                    st.pending_R = None
                for j in range(max(0, jlast + 1 - DELAY), jlast + 1):
                    emit_PV(j)
                if me == "scalar":
                    c.wait(S.b_pv, pair_pv_base + jlast + 1)
                    i = eng.activation(rr_sb[rslanes[0]:rslanes[0] + 1, par * 512:par * 512 + 512],
                                       ps_main[rslanes[0]:rslanes[0] + 1, 3072:3584], AF.Ln)
                    c.inc(S.b_nrm, i)
                    c.wait(S.b_nrm)
                    i = eng.activation(rr_sb[rslanes[0]:rslanes[0] + 1, par * 512:par * 512 + 512],
                                       rr_sb[rslanes[0]:rslanes[0] + 1, par * 512:par * 512 + 512],
                                       AF.Exp, scale=-1.0)
                    c.inc(S.b_nrm, i)
                else:
                    c.inc(S.b_nrm, None, 2)
                if me == "vector":
                    c.wait(S.b_pv, pair_pv_base + jlast + 1)
                    i = eng.reciprocal(rr_sb[rslanes[1]:rslanes[1] + 1, par * 512:par * 512 + 512],
                                       ps_main[rslanes[1]:rslanes[1] + 1, 3072:3584])
                    c.inc(S.b_nrm, i)
                else:
                    c.inc(S.b_nrm, None)
                st.rsfree[par] = S.b_nrm.n
                st.pending_R = (pp, S.b_nrm.n)
            if st.pending_R is not None:
                emit_R(*st.pending_R)
                st.pending_R = None

        def stage_c(qt, bnd_io=None):
            for t in range(4):
                buf = t % 2
                if me == "sync":
                    c.wait(S.a_xn)
                    c.wait(S.c_x1, st.x1ev_free[buf])
                    i = eng.dma_start(xln_sb[:, buf * E:(buf + 1) * E],
                                      x_d[(qt * 4 + t) * 128:(qt * 4 + t + 1) * 128, :])
                    c.inc(S.c_xr, i, 16)
                else:
                    c.inc(S.c_xr, None, 16)
                if me == "tensor":
                    if t == 0:
                        c.wait(S.b_nrm)
                        c.wait(S.b_exp)
                    c.wait(S.c_x1, st.x1ev_free[buf])
                    poff = buf * 1024
                    i = None
                    for k in range(8):
                        for h in range(2):
                            i = eng.matmul(ps_main[:, poff + h * 512:poff + (h + 1) * 512],
                                           att_sb[:, k * 512 + t * 128:k * 512 + t * 128 + 128],
                                           wproj_sb[:, k * E + h * 512:k * E + (h + 1) * 512],
                                           start=(k == 0), stop=False)
                    for h in range(2):
                        i = eng.matmul(ps_main[:, poff + h * 512:poff + (h + 1) * 512],
                                       onesrow_sb[0:1, :],
                                       bprojt_sb[0:1, h * 512:(h + 1) * 512],
                                       start=False, stop=True)
                    c.inc(S.c_prj, i)
                else:
                    c.inc(S.c_prj, None)
                if me == "vector":
                    c.wait(S.c_prj)
                    c.wait(S.c_xr)
                    c.wait(S.c_st, st.x1st_free)
                    poff = buf * 1024
                    i = eng.tensor_tensor(x1_sb[:, :], ps_main[:, poff:poff + 1024],
                                          xln_sb[:, buf * E:(buf + 1) * E], ALU.add)
                    c.inc(S.c_x1, i)
                else:
                    c.inc(S.c_x1, None)
                st.x1ev_free[buf] = S.c_x1.n
                if me == "gpsimd":
                    c.wait(S.c_x1)
                    i = eng.dma_start(x1_scr[(qt * 4 + t) * 128:(qt * 4 + t + 1) * 128, :],
                                      x1_sb[:, :])
                    c.inc(S.c_st, i, 16)
                else:
                    c.inc(S.c_st, None, 16)
                st.x1st_free = S.c_st.n
            st.stageC_free = S.c_x1.n
            st.prj_all_prev = S.c_prj.n
            st.xn_free = [S.a_xn.n, S.a_xn.n]

        prep_io(0)
        prep_ln(0)
        prep_tr(0)
        for qt in range(NQT):
            for s4 in range(4):
                g = qt * 4 + s4
                if g + 1 < NSUB and (g + 1) % 4 != 0:
                    prep_io(g + 1)
                    prep_ln(g + 1)
                heavy(g)
            stage_b(qt)
            stage_c(qt)
            if qt + 1 < NQT:
                g2 = 4 * (qt + 1)
                prep_io(g2)
                prep_ln(g2)
                prep_tr(g2)

    run_phase(p1)
    attn.close()

    mid = ExitStack()
    c_repl = sb(mid, "c_repl", [128, 10], F32)
    idx_sb = sb(mid, "idx_sb", [1, 2], mybir.dt.int32)
    cb2_repl = sb(mid, "cb2_repl", [128, E], F32)
    p23 = ExitStack()
    x1ln_sb = sb(p23, "x1ln_sb", [128, 3 * E], F32)
    ln2_sb = sb(p23, "ln2_sb", [128, 24], F32)
    xn1f_sb = sb(p23, "xn1f_sb", [128, 2 * E], F32)
    xn1b_sb = sb(p23, "xn1b_sb", [128, 2 * E], BF16)
    xn1t_sb = sb(p23, "xn1t_sb", [128, 2 * E], BF16)
    sqd_sb = sb(p23, "sqd_sb", [128, E], BF16)
    gatew_sb = sb(p23, "gatew_sb", [128, NE * 8], F32)
    seqrow_sb = sb(p23, "seqrow_sb", [1, E], F32)
    seqst_sb = sb(p23, "seqst_sb", [128, 8], F32)
    prob_sb = sb(p23, "prob_sb", [1, 80], F32)
    ccol_sb = sb(p23, "ccol_sb", [8, 1], F32)
    b2_sb = sb(p23, "b2_sb", [8, E], F32)
    cb2row_sb = sb(p23, "cb2row_sb", [1, E], F32)
    iota8_sb = sb(p23, "iota8_sb", [1, 8], F32)

    def p2(c):
        me, eng = c.role, c.eng
        if me == "sync":
            c.wait(S.c_st)
            c.inc(S.e_ld, eng.dma_start(gatew_sb[:, :],
                                        AP(gatew_d.tensor, 0, [[NE, 128], [128 * NE, 8], [1, NE]])), 16)
            c.inc(S.e_ld, eng.dma_start(b2_sb[:, :], b2_d), 16)
            c.inc(S.e_ld, eng.dma_start(iota8_sb[:, :], iota8_d), 16)
        else:
            c.inc(S.e_ld, None, 48)
        st2 = type("st2", (), {})()
        st2.xnf_free = [0, 0, 0]
        st2.seq_prev = 0
        st2.seq_prev2 = 0
        st2.tr_prev = 0
        st2.tr_prev2 = 0
        st2.st_prev = 0
        st2.st_prev2 = 0
        st2.lnv = {}
        st2.lns = {}

        def pre(g):
            buf = g % 3
            lb = buf * 8
            if me == "sync":
                c.wait(S.d_xnf, st2.xnf_free[buf])
                i = eng.dma_start(x1ln_sb[:, buf * E:(buf + 1) * E],
                                  x1_scr[g * 128:(g + 1) * 128, :])
                c.inc(S.d_x, i, 16)
            else:
                c.inc(S.d_x, None, 16)
            if me == "vector":
                c.wait(S.d_x, 16 * (g + 1))
                xsrc = x1ln_sb[:, buf * E:(buf + 1) * E]
                i = eng.reduce_sum(ln2_sb[:, lb:lb + 1], xsrc, axis=AX.X, negate=True)
                c.inc(S.d_lnv, i)
                i = eng.scalar_tensor_tensor(sqd_sb[:, :], xsrc, 1.0, xsrc,
                                             ALU.mult, ALU.mult, accum_out=ln2_sb[:, lb + 6:lb + 7])
                c.inc(S.d_lnv, i)
                c.wait(S.d_lnv)
                i = eng.tensor_scalar(ln2_sb[:, lb + 1:lb + 2], ln2_sb[:, lb:lb + 1],
                                      1.0 / E, None, ALU.mult)
                c.inc(S.d_lnv, i)
                c.wait(S.d_lnv)
                i = eng.tensor_tensor(ln2_sb[:, lb + 2:lb + 3], ln2_sb[:, lb + 1:lb + 2],
                                      ln2_sb[:, lb + 1:lb + 2], ALU.mult)
                c.inc(S.d_lnv, i)
                c.wait(S.d_lnv)
                i = eng.tensor_scalar(ln2_sb[:, lb + 3:lb + 4], ln2_sb[:, lb + 6:lb + 7],
                                      1.0 / E, ln2_sb[:, lb + 2:lb + 3], ALU.mult, ALU.subtract)
                c.inc(S.d_lnv, i)
            else:
                c.inc(S.d_lnv, None, 5)
            st2.lnv[g] = S.d_lnv.n
            if me == "scalar":
                c.wait(S.d_lnv, st2.lnv[g])
                i = eng.activation(ln2_sb[:, lb + 4:lb + 5], ln2_sb[:, lb + 3:lb + 4],
                                   AF.Ln, bias=cst_sb[:, 0:1])
                c.inc(S.d_lns, i)
                c.wait(S.d_lns)
                i = eng.activation(ln2_sb[:, lb + 5:lb + 6], ln2_sb[:, lb + 4:lb + 5],
                                   AF.Exp, scale=-0.5)
                c.inc(S.d_lns, i)
            else:
                c.inc(S.d_lns, None, 2)
            st2.lns[g] = S.d_lns.n

        def post(g):
            buf = g % 3
            lb = buf * 8
            fb = (g % 2) * E
            xsrc = x1ln_sb[:, buf * E:(buf + 1) * E]
            if me == "vector":
                c.wait(S.d_lns, st2.lns[g])
                c.wait(S.d_sum, st2.seq_prev2)
                i = eng.tensor_scalar(xn1f_sb[:, fb:fb + E], xsrc, ln2_sb[:, lb + 1:lb + 2],
                                      ln2_sb[:, lb + 5:lb + 6], ALU.add, ALU.mult)
                c.inc(S.d_xnf, i)
                c.wait(S.d_tr, st2.tr_prev2)
                i = eng.tensor_copy(xn1b_sb[:, fb:fb + E], xn1f_sb[:, fb:fb + E])
                c.inc(S.d_xnb, i)
            else:
                c.inc(S.d_xnf, None)
                c.inc(S.d_xnb, None)
            st2.xnf_free[buf] = S.d_xnf.n
            if me == "tensor":
                c.wait(S.d_xnf)
                i = None
                for h in range(2):
                    i = eng.matmul(ps_main[0:1, 2560 + h * 512:2560 + (h + 1) * 512],
                                   ones_sb[:, 0:1], xn1f_sb[:, fb + h * 512:fb + (h + 1) * 512],
                                   start=(g == 0), stop=(g == NSUB - 1))
                c.inc(S.d_sum, i)
            else:
                c.inc(S.d_sum, None)
            st2.seq_prev2 = st2.seq_prev
            st2.seq_prev = S.d_sum.n
            if me == "tensor":
                c.wait(S.d_xnb)
                c.wait(S.d_trev)
                i = None
                for ch in range(8):
                    i = eng.transpose(ps_tr[:, ch * 128:(ch + 1) * 128],
                                      xn1b_sb[:, fb + ch * 128:fb + (ch + 1) * 128], ident_sb[:, :])
                c.inc(S.d_tr, i)
            else:
                c.inc(S.d_tr, None)
            st2.tr_prev2 = st2.tr_prev
            st2.tr_prev = S.d_tr.n
            if me == "vector":
                c.wait(S.d_tr, st2.tr_prev)
                c.wait(S.d_st, st2.st_prev2)
                c.inc(S.d_trev, eng.tensor_copy(xn1t_sb[:, fb:fb + E], ps_tr[:, :]))
            else:
                c.inc(S.d_trev, None)
            if me == "gpsimd":
                c.wait(S.d_trev, S.d_trev.n)
                dst = AP(xn1t_scr.tensor, g * 128, [[T, 128], [128 * T, 8], [1, 128]])
                c.inc(S.d_st, eng.dma_start(dst, xn1t_sb[:, fb:fb + E]), 16)
            else:
                c.inc(S.d_st, None, 16)
            st2.st_prev2 = st2.st_prev
            st2.st_prev = S.d_st.n

        pre(0)
        pre(1)
        for g in range(NSUB):
            post(g)
            if g + 2 < NSUB:
                pre(g + 2)

    run_phase(p2)

    def p3(c):
        me, eng = c.role, c.eng

        def vstep(fn):
            if me == "vector":
                c.inc(S.e_v, fn())
                c.wait(S.e_v)
            else:
                c.inc(S.e_v, None)

        def sstep(fn):
            if me == "scalar":
                c.inc(S.e_s, fn())
                c.wait(S.e_s)
            else:
                c.inc(S.e_s, None)

        if me == "vector":
            c.wait(S.d_sum)
        vstep(lambda: eng.tensor_copy(seqrow_sb[0:1, :], ps_main[0:1, 2560:3584]))
        if me == "sync":
            c.wait(S.e_v)
            c.inc(S.e_ld, eng.dma_start(vec_scr, seqrow_sb[0:1, :]), 16)
            c.wait(S.e_ld)
            c.inc(S.e_ld, eng.dma_start(seqst_sb[:, :], AP(vec_scr.tensor, 0, [[1, 128], [128, 8]]), allow_slow_non_contiguous=True), 16)
        else:
            c.inc(S.e_ld, None, 32)
        if me == "tensor":
            c.wait(S.e_ld)
            i = None
            for k in range(8):
                i = eng.matmul(ps_main[0:1, 0:8], seqst_sb[:, k:k + 1],
                               gatew_sb[:, k * 8:(k + 1) * 8], start=(k == 0), stop=(k == 7))
            c.inc(S.e_t, i)
        else:
            c.inc(S.e_t, None)
        if me == "scalar":
            c.wait(S.e_t)
        sstep(lambda: eng.activation(prob_sb[0:1, 0:8], ps_main[0:1, 0:8], AF.Copy, scale=1.0 / T))
        if me == "vector":
            c.wait(S.e_s)
        vstep(lambda: eng.reduce_max(prob_sb[0:1, 8:9], prob_sb[0:1, 0:8], axis=AX.X, negate=True))
        if me == "scalar":
            c.wait(S.e_v)
        sstep(lambda: eng.activation(prob_sb[0:1, 16:24], prob_sb[0:1, 0:8], AF.Exp,
                                     bias=prob_sb[0:1, 8:9]))
        if me == "vector":
            c.wait(S.e_s)
        vstep(lambda: eng.reduce_sum(prob_sb[0:1, 9:10], prob_sb[0:1, 16:24], axis=AX.X))
        vstep(lambda: eng.reciprocal(prob_sb[0:1, 10:11], prob_sb[0:1, 9:10]))
        vstep(lambda: eng.tensor_scalar(prob_sb[0:1, 24:32], prob_sb[0:1, 16:24],
                                        prob_sb[0:1, 10:11], None, ALU.mult))
        vstep(lambda: eng.reduce_max(prob_sb[0:1, 11:12], prob_sb[0:1, 24:32], axis=AX.X))
        vstep(lambda: eng.tensor_scalar(prob_sb[0:1, 32:40], prob_sb[0:1, 24:32],
                                        prob_sb[0:1, 11:12], None, ALU.is_equal))
        vstep(lambda: eng.tensor_tensor(prob_sb[0:1, 40:48], prob_sb[0:1, 24:32],
                                        prob_sb[0:1, 32:40], ALU.mult))
        vstep(lambda: eng.tensor_tensor(prob_sb[0:1, 48:56], prob_sb[0:1, 24:32],
                                        prob_sb[0:1, 40:48], ALU.subtract))
        vstep(lambda: eng.reduce_max(prob_sb[0:1, 12:13], prob_sb[0:1, 48:56], axis=AX.X))
        vstep(lambda: eng.tensor_scalar(prob_sb[0:1, 56:64], prob_sb[0:1, 48:56],
                                        prob_sb[0:1, 12:13], None, ALU.is_equal))
        vstep(lambda: eng.tensor_tensor(prob_sb[0:1, 13:14], prob_sb[0:1, 11:12],
                                        prob_sb[0:1, 12:13], ALU.add))
        vstep(lambda: eng.reciprocal(prob_sb[0:1, 14:15], prob_sb[0:1, 13:14]))
        vstep(lambda: eng.tensor_tensor(prob_sb[0:1, 40:48], prob_sb[0:1, 32:40],
                                        iota8_sb[0:1, :], ALU.mult))
        vstep(lambda: eng.reduce_sum(prob_sb[0:1, 74:75], prob_sb[0:1, 40:48], axis=AX.X))
        vstep(lambda: eng.tensor_tensor(prob_sb[0:1, 40:48], prob_sb[0:1, 56:64],
                                        iota8_sb[0:1, :], ALU.mult))
        vstep(lambda: eng.reduce_sum(prob_sb[0:1, 75:76], prob_sb[0:1, 40:48], axis=AX.X))
        vstep(lambda: eng.tensor_copy(idx_sb[0:1, :], prob_sb[0:1, 74:76]))
        vstep(lambda: eng.tensor_copy(prob_sb[0:1, 72:73], prob_sb[0:1, 11:12]))
        vstep(lambda: eng.tensor_copy(prob_sb[0:1, 73:74], prob_sb[0:1, 12:13]))
        vstep(lambda: eng.tensor_scalar(prob_sb[0:1, 72:74], prob_sb[0:1, 72:74],
                                        prob_sb[0:1, 14:15], None, ALU.mult))
        vstep(lambda: eng.tensor_scalar(prob_sb[0:1, 32:40], prob_sb[0:1, 32:40],
                                        prob_sb[0:1, 11:12], None, ALU.mult))
        vstep(lambda: eng.tensor_scalar(prob_sb[0:1, 56:64], prob_sb[0:1, 56:64],
                                        prob_sb[0:1, 12:13], None, ALU.mult))
        vstep(lambda: eng.tensor_tensor(prob_sb[0:1, 64:72], prob_sb[0:1, 32:40],
                                        prob_sb[0:1, 56:64], ALU.add))
        vstep(lambda: eng.tensor_scalar(prob_sb[0:1, 64:72], prob_sb[0:1, 64:72],
                                        prob_sb[0:1, 14:15], None, ALU.mult))
        if me == "sync":
            c.wait(S.e_v)
            c.inc(S.e_ld, eng.dma_start(AP(vec_scr.tensor, 0, [[NE, 1], [1, NE]]),
                                        prob_sb[0:1, 64:72]), 16)
            c.wait(S.e_ld)
            c.inc(S.e_ld, eng.dma_start(ccol_sb[:, :], AP(vec_scr.tensor, 0, [[1, 8], [1, 1]]), allow_slow_non_contiguous=True), 16)
        else:
            c.inc(S.e_ld, None, 32)
        if me == "tensor":
            c.wait(S.e_ld)
            i = None
            for h in range(2):
                i = eng.matmul(ps_main[0:1, 1024 + h * 512:1024 + (h + 1) * 512],
                               ccol_sb[:, 0:1], b2_sb[:, h * 512:(h + 1) * 512],
                               start=True, stop=True)
            c.inc(S.e_t, i)
        else:
            c.inc(S.e_t, None)
        if me == "vector":
            c.wait(S.e_t)
        vstep(lambda: eng.tensor_copy(cb2row_sb[0:1, :], ps_main[0:1, 1024:2048]))
        if me == "tensor":
            c.wait(S.e_v)
            for h in range(2):
                eng.matmul(ps_main[:, h * 512:(h + 1) * 512], ones_sb[0:1, 0:128],
                           cb2row_sb[0:1, h * 512:(h + 1) * 512], start=True, stop=True)
            i = eng.matmul(ps_main[:, 2048:2058], ones_sb[0:1, 0:128],
                           prob_sb[0:1, 64:74], start=True, stop=True)
            c.inc(S.e_t, i)
        else:
            c.inc(S.e_t, None)
        if me == "vector":
            c.wait(S.e_t)
        vstep(lambda: eng.tensor_copy(cb2_repl[:, :], ps_main[:, 0:1024]))
        vstep(lambda: eng.tensor_copy(c_repl[:, :], ps_main[:, 2048:2058]))
        if dbg_d is not None:
            if me == "sync":
                c.wait(S.e_v)
                c.inc(S.e_ld, eng.dma_start(dbg_d, prob_sb[0:1, :]), 16)
                c.wait(S.e_ld)
            else:
                c.inc(S.e_ld, None, 16)

    run_phase(p3)
    p23.close()

    ffn = ExitStack()
    w1_sb = sb(ffn, "w1_sb", [128, 8 * FF], BF16)
    w2_sb = sb(ffn, "w2_sb", [128, 32 * E], BF16)
    h1t_sb = sb(ffn, "h1t_sb", [128, 32 * TCH], BF16)
    xn1tc_sb = sb(ffn, "xn1tc_sb", [128, 2 * 8 * TCH], BF16)
    resid_sb = sb(ffn, "resid_sb", [128, 4 * E], F32)
    outb_sb = sb(ffn, "outb_sb", [128, 4 * E], F32)
    b1t_sb = sb(ffn, "b1t_sb", [128, 32], F32)

    NSLOT = 2

    def p4(c):
        from concourse.expressions import RuntimeValue
        me, eng = c.role, c.eng
        h1done_prev = 0
        ydone_prev = 0
        geldone_prev = 0
        xnc_free = [0, 0]
        res_free = [0, 0]
        outb_free = [0, 0]
        gel_free = [0, 0]
        yps_free = [0, 0]
        ych_prev = 0
        if me == "sync":
            c.wait(S.d_st)
        for e in range(NSLOT):
            src_scr = x1_scr if e == 0 else part_scr[(e - 1) % 2]
            dst_scr = out_d if e == NSLOT - 1 else part_scr[e % 2]
            if me == "gpsimd":
                regs = eng.alloc_register(f"p4w{e}")
                regb = eng.alloc_register(f"p4b{e}")
                eng.reg_load(regs, idx_sb[0:1, e:e + 1])
                eng.reg_mov(regb, regs)
                eng.reg_alu(regs, regs, E * FF, ALU.mult)
                eng.reg_alu(regb, regb, FF, ALU.mult)
                offw = RuntimeValue(regs, 0, (NE - 1) * E * FF)
                offb = RuntimeValue(regb, 0, (NE - 1) * FF)
                c.wait(S.f_h1, h1done_prev)
                src = AP(w1_d.tensor, offw, [[FF, 128], [128 * FF, 8], [1, FF]])
                c.inc(S.f_w, eng.dma_start(w1_sb[:, :], src), 16)
                c.wait(S.f_y, ydone_prev)
                src = AP(w2_d.tensor, offw, [[E, 128], [128 * E, 32], [1, E]])
                c.inc(S.f_w, eng.dma_start(w2_sb[:, :], src), 16)
                c.wait(S.f_gel, geldone_prev)
                src = AP(b1_d.tensor, offb, [[1, 128], [128, 32]])
                c.inc(S.f_w, eng.dma_start(b1t_sb[:, :], src, allow_slow_non_contiguous=True), 16)
            else:
                c.inc(S.f_w, None, 48)
            for ch in range(NCH):
                cbuf = ch % 2
                if me == "sync":
                    c.wait(S.f_h1, xnc_free[cbuf])
                    src = AP(xn1t_scr.tensor, ch * TCH, [[T, 128], [128 * T, 8], [1, TCH]])
                    c.inc(S.f_xn, eng.dma_start(xn1tc_sb[:, cbuf * 8 * TCH:(cbuf + 1) * 8 * TCH], src), 16)
                    c.wait(S.f_ev, res_free[cbuf])
                    if e > 0:
                        c.wait(S.f_st, 16 * ((e - 1) * NCH + ch + 1))
                    src = AP(src_scr.tensor, ch * TCH * E, [[E, 128], [128 * E, 2], [1, E]])
                    c.inc(S.f_res, eng.dma_start(resid_sb[:, cbuf * 2 * E:(cbuf + 1) * 2 * E], src), 16)
                else:
                    c.inc(S.f_xn, None, 16)
                    c.inc(S.f_res, None, 16)
                for m in range(32):
                    if me == "tensor":
                        if m == 0:
                            c.wait(S.f_w, 16 * (3 * e + 1))
                            c.wait(S.f_xn, 16 * (e * NCH + ch + 1))
                        c.wait(S.f_gel, gel_free[m % 2])
                        hoff = (m % 2) * 512
                        i = None
                        for k in range(8):
                            i = eng.matmul(ps_main[:, hoff:hoff + TCH],
                                           w1_sb[:, k * FF + m * 128:k * FF + (m + 1) * 128],
                                           xn1tc_sb[:, cbuf * 8 * TCH + k * TCH:cbuf * 8 * TCH + (k + 1) * TCH],
                                           start=(k == 0), stop=(k == 7))
                        c.inc(S.f_h1, i)
                    else:
                        c.inc(S.f_h1, None)
                    if me == "scalar":
                        c.wait(S.f_h1)
                        if m == 0:
                            c.wait(S.f_w, 16 * (3 * e + 3))
                            c.wait(S.f_y, ych_prev)
                        hoff = (m % 2) * 512
                        i = eng.activation(h1t_sb[:, m * TCH:(m + 1) * TCH],
                                           ps_main[:, hoff:hoff + TCH], AF.Gelu,
                                           bias=b1t_sb[:, m:m + 1])
                        c.inc(S.f_gel, i)
                    else:
                        c.inc(S.f_gel, None)
                    gel_free[m % 2] = S.f_gel.n
                xnc_free[cbuf] = S.f_h1.n
                if ch == NCH - 1:
                    h1done_prev = S.f_h1.n
                    geldone_prev = S.f_gel.n
                for t in range(2):
                    yoff = 1024 + t * 1024
                    if me == "tensor":
                        c.wait(S.f_gel)
                        if ch == 0 and t == 0:
                            c.wait(S.f_w, 16 * (3 * e + 2))
                        c.wait(S.f_ev, yps_free[t])
                        i = None
                        for k2 in range(32):
                            for h in range(2):
                                i = eng.matmul(ps_main[:, yoff + h * 512:yoff + (h + 1) * 512],
                                               h1t_sb[:, k2 * TCH + t * 128:k2 * TCH + t * 128 + 128],
                                               w2_sb[:, k2 * E + h * 512:k2 * E + (h + 1) * 512],
                                               start=(k2 == 0), stop=(k2 == 31))
                        c.inc(S.f_y, i)
                    else:
                        c.inc(S.f_y, None)
                    if me == "vector":
                        c.wait(S.f_y)
                        c.wait(S.f_res)
                        c.wait(S.f_st, outb_free[cbuf])
                        ob = outb_sb[:, cbuf * 2 * E + t * E:cbuf * 2 * E + (t + 1) * E]
                        i = eng.scalar_tensor_tensor(ob,
                                                     ps_main[:, yoff:yoff + 1024],
                                                     c_repl[:, 8 + e:9 + e],
                                                     resid_sb[:, cbuf * 2 * E + t * E:cbuf * 2 * E + (t + 1) * E],
                                                     ALU.mult, ALU.add)
                        if e == 0:
                            i = eng.tensor_tensor(ob, ob, cb2_repl[:, :], ALU.add)
                        c.inc(S.f_ev, i)
                    else:
                        c.inc(S.f_ev, None)
                    yps_free[t] = S.f_ev.n
                if ch == NCH - 1:
                    ydone_prev = S.f_y.n
                ych_prev = S.f_y.n
                res_free[cbuf] = S.f_ev.n
                if me == "gpsimd":
                    c.wait(S.f_ev)
                    dst = AP(dst_scr.tensor, ch * TCH * E, [[E, 128], [128 * E, 2], [1, E]])
                    c.inc(S.f_st, eng.dma_start(dst, outb_sb[:, cbuf * 2 * E:(cbuf + 1) * 2 * E]), 16)
                else:
                    c.inc(S.f_st, None, 16)
                outb_free[cbuf] = S.f_st.n
        if me == "sync":
            c.wait(S.f_st)

    run_phase(p4)
    ffn.close()
    mid.close()
    glob.close()
    ctxstack.close()
    return nc


_NC_CACHE = None


def kernel(L0_attn_x, L0_attn_W_kqv, L0_attn_W_proj, L0_attn_b_proj,
           L0_ffn_gate_w, L0_ffn_w1, L0_ffn_b1, L0_ffn_w2, L0_ffn_b2):
    global _NC_CACHE
    if _NC_CACHE is None:
        _NC_CACHE = build()
    nc = _NC_CACHE
    x = np.ascontiguousarray(np.asarray(L0_attn_x, dtype=np.float32))
    common = {
        "wkqv": np.ascontiguousarray(np.asarray(L0_attn_W_kqv, dtype=np.float32)),
        "wproj": np.ascontiguousarray(np.asarray(L0_attn_W_proj, dtype=np.float32)),
        "bproj": np.ascontiguousarray(np.asarray(L0_attn_b_proj, dtype=np.float32)),
        "gatew": np.ascontiguousarray(np.asarray(L0_ffn_gate_w, dtype=np.float32)),
        "w1": np.ascontiguousarray(np.asarray(L0_ffn_w1, dtype=np.float32)),
        "b1": np.ascontiguousarray(np.asarray(L0_ffn_b1, dtype=np.float32)),
        "w2": np.ascontiguousarray(np.asarray(L0_ffn_w2, dtype=np.float32)),
        "b2": np.ascontiguousarray(np.asarray(L0_ffn_b2, dtype=np.float32)),
    }
    in_maps = [dict(common, x=np.ascontiguousarray(x[b])) for b in range(8)]
    res = run_bass_kernel_spmd(nc, in_maps, core_ids=list(range(8)))
    return np.stack([res.results[b]["out"] for b in range(8)], axis=0)


# version 17 (final)
# speedup vs baseline: 3.1372x; 1.0336x over previous
import numpy as np
import ml_dtypes
from contextlib import ExitStack

import concourse.bass as bass
import concourse.mybir as mybir
from concourse.bass_utils import run_bass_kernel_spmd

F32 = mybir.dt.float32
BF16 = mybir.dt.bfloat16
AF = mybir.ActivationFunctionType
ALU = mybir.AluOpType
AX = mybir.AxisListType
AP = bass.AP

T, E, NH, HD = 4096, 1024, 16, 64
NP = NH // 2
QTW, NQT = 512, T // 512
NSUB = T // 128
FF, NE = 4096, 8
TCH, NCH = 256, T // 256
EPS = 1e-5
EXPB = -12.0


class Sem:
    __slots__ = ("h", "n")

    def __init__(self, h):
        self.h = h
        self.n = 0


class C:

    def __init__(self, role, eng, sems):
        self.role = role
        self.eng = eng
        self.sems = sems

    def inc(self, sem, instr, k=1):
        sem.n += k
        if instr is not None:
            instr.then_inc(sem.h, k)

    def wait(self, sem, n=None):
        v = sem.n if n is None else n
        if v > 0:
            self.eng.wait_ge(sem.h, v)


def build(debug=False):
    nc = bass.Bass("TRN2", target_bir_lowering=False)
    dbgkind = "ExternalOutput" if debug else "Internal"

    x_d = nc.dram_tensor("x", [T, E], F32, kind="ExternalInput").ap()
    wkqv_d = nc.dram_tensor("wkqv", [E, 3 * E], F32, kind="ExternalInput").ap()
    wproj_d = nc.dram_tensor("wproj", [E, E], F32, kind="ExternalInput").ap()
    bproj_d = nc.dram_tensor("bproj", [E], F32, kind="ExternalInput").ap()
    gatew_d = nc.dram_tensor("gatew", [E, NE], F32, kind="ExternalInput").ap()
    w1_d = nc.dram_tensor("w1", [NE, E, FF], F32, kind="ExternalInput").ap()
    b1_d = nc.dram_tensor("b1", [NE, FF], F32, kind="ExternalInput").ap()
    w2_d = nc.dram_tensor("w2", [NE, FF, E], F32, kind="ExternalInput").ap()
    b2_d = nc.dram_tensor("b2", [NE, E], F32, kind="ExternalInput").ap()
    out_d = nc.dram_tensor("out", [T, E], F32, kind="ExternalOutput").ap()
    dbg_d = nc.dram_tensor("dbg", [1, 80], F32, kind="ExternalOutput").ap() if debug else None
    if debug:
        dbg_kt = nc.dram_tensor("dbg_kt", [128, NP * T], BF16, kind="ExternalOutput").ap()
        dbg_v = nc.dram_tensor("dbg_v", [128, NSUB * E], BF16, kind="ExternalOutput").ap()
        dbg_qt = nc.dram_tensor("dbg_qt", [128, NP * QTW], BF16, kind="ExternalOutput").ap()
        dbg_att = nc.dram_tensor("dbg_att", [128, NP * QTW], BF16, kind="ExternalOutput").ap()

    wkqv_scr = nc.dram_tensor("wkqv_scr", [E, 3 * E], BF16, kind="Internal").ap()
    x1_scr = nc.dram_tensor("x1_scr", [T, E], F32, kind=dbgkind).ap()
    xn1t_scr = nc.dram_tensor("xn1t_scr", [E, T], BF16, kind=dbgkind).ap()
    part_scr = [nc.dram_tensor(f"part{i}", [T, E], F32, kind="Internal").ap() for i in range(2)]
    vec_scr = nc.dram_tensor("vec_scr", [E], F32, kind="Internal").ap()

    ident_d = nc.inline_tensor(np.asarray(np.eye(128, dtype=ml_dtypes.bfloat16)), name="identc").ap()
    ones_d = nc.inline_tensor(np.ones((128, 128), dtype=np.float32), name="onesc").ap()
    onesb_d = nc.inline_tensor(np.asarray(np.ones((128, 2), dtype=ml_dtypes.bfloat16)), name="onesbc").ap()
    uu = np.arange(128)[None, :]
    pp_ = np.arange(128)[:, None]
    tri_d = nc.inline_tensor(np.asarray((pp_ <= uu).astype(ml_dtypes.bfloat16)), name="tric").ap()
    onesrow_d = nc.inline_tensor(np.asarray(np.ones((1, 128), dtype=ml_dtypes.bfloat16)), name="onesrowc").ap()
    iota8_d = nc.inline_tensor(np.arange(8, dtype=np.float32)[None, :], name="iota8c").ap()

    sems = {}
    ctxstack = ExitStack()

    def mksem(name):
        s = Sem(ctxstack.enter_context(nc.semaphore(name)))
        sems[name] = s
        return s

    S = type("S", (), {})()
    for nm in ["p0_ld", "p0_t", "p0_v", "p0_st",
               "a_x", "a_slab", "a_slab2", "a_tslab", "a_lnv", "a_lns", "a_xn",
               "a_tr", "a_trev", "a_kqv", "a_kqst", "a_vev",
               "b_s", "b_exp", "b_msk", "b_pv", "b_nrm", "b_rmm",
               "c_xr", "c_prj", "c_x1", "c_st",
               "d_x", "d_lnv", "d_lns", "d_xnf", "d_xnb", "d_tr", "d_trev", "d_sum", "d_st",
               "e_ld", "e_t", "e_s", "e_v",
               "f_w", "f_xn", "f_res", "f_h1", "f_gel", "f_y", "f_ev", "f_st"]:
        setattr(S, nm, mksem(nm))

    glob = ExitStack()

    def sb(stack, name, shape, dt):
        return stack.enter_context(nc.sbuf_tensor(name, shape, dt))

    ident_sb = sb(glob, "ident_sb", [128, 128], BF16)
    ones_sb = sb(glob, "ones_sb", [128, 128], F32)
    onesb_sb = sb(glob, "onesb_sb", [128, 2], BF16)
    ln_sb = sb(glob, "ln_sb", [128, 16], F32)
    cst_sb = sb(glob, "cst_sb", [128, 2], F32)
    ps_main = glob.enter_context(nc.psum_tensor("ps_main", [128, 3584], F32))
    ps_tr = glob.enter_context(nc.psum_tensor("ps_tr", [128, 1024], BF16))

    def run_phase(fn):
        saved = {k: s.n for k, s in sems.items()}
        with nc.Block() as blk:
            for role, starter in (("sync", blk.sync), ("gpsimd", blk.gpsimd),
                                  ("tensor", blk.tensor), ("vector", blk.vector),
                                  ("scalar", blk.scalar)):
                for k, s in sems.items():
                    s.n = saved[k]

                def body(eng, role=role):
                    fn(C(role, eng, sems))

                starter(body)

    def ln_stats(c, src_ap, sqdump_ap, s_v, s_s):
        me, eng = c.role, c.eng
        if me == "vector":
            c.inc(s_v, eng.reduce_sum(ln_sb[:, 0:1], src_ap, axis=AX.X, negate=True))
            c.inc(s_v, eng.scalar_tensor_tensor(sqdump_ap, src_ap, 1.0, src_ap,
                                                ALU.mult, ALU.mult, accum_out=ln_sb[:, 6:7]))
        else:
            c.inc(s_v, None)
            c.inc(s_v, None)
        if me == "scalar":
            c.wait(s_v)
            c.inc(s_s, eng.activation(ln_sb[:, 1:2], ln_sb[:, 0:1], AF.Copy, scale=1.0 / E))
            c.wait(s_s)
            c.inc(s_s, eng.activation(ln_sb[:, 2:3], ln_sb[:, 1:2], AF.Square))
        else:
            c.inc(s_s, None)
            c.inc(s_s, None)
        if me == "vector":
            c.wait(s_s)
            c.inc(s_v, eng.tensor_scalar(ln_sb[:, 3:4], ln_sb[:, 6:7], 1.0 / E, ln_sb[:, 2:3],
                                         ALU.mult, ALU.subtract))
        else:
            c.inc(s_v, None)
        if me == "scalar":
            c.wait(s_v)
            c.inc(s_s, eng.activation(ln_sb[:, 4:5], ln_sb[:, 3:4], AF.Ln, bias=cst_sb[:, 0:1]))
            c.wait(s_s)
            c.inc(s_s, eng.activation(ln_sb[:, 5:6], ln_sb[:, 4:5], AF.Exp, scale=-0.5))
        else:
            c.inc(s_s, None)
            c.inc(s_s, None)

    attn = ExitStack()
    kt_sb = sb(attn, "kt_sb", [128, NP * T], BF16)
    v_sb = sb(attn, "v_sb", [128, NSUB * E], BF16)
    qt_sb = sb(attn, "qt_sb", [128, NP * QTW], BF16)
    slab_sb = sb(attn, "slab_sb", [128, 3 * 3072], BF16)
    wproj_sb = sb(attn, "wproj_sb", [128, 8 * E], BF16)
    tri_sb = sb(attn, "tri_sb", [128, 128], BF16)
    onesrow_sb = sb(attn, "onesrow_sb", [1, 128], BF16)
    bprojt_sb = sb(attn, "bprojt_sb", [1, E], BF16)
    xln_sb = sb(attn, "xln_sb", [128, 2 * E], F32)
    xn_sb = sb(attn, "xn_sb", [128, E], BF16)
    xnt_sb = sb(attn, "xnt_sb", [128, E], BF16)
    kq_sb = sb(attn, "kq_sb", [128, E], BF16)
    pt_sb = sb(attn, "pt_sb", [128, 2048], BF16)
    att_sb = sb(attn, "att_sb", [128, NP * QTW], BF16)
    rr_sb = sb(attn, "rr_sb", [128, 1024], F32)
    x1_sb = sb(attn, "x1_sb", [128, E], F32)

    def p0(c):
        me, eng = c.role, c.eng
        if me == "sync":
            for dst, srcd in ((ident_sb[:, :], ident_d), (ones_sb[:, :], ones_d),
                              (onesb_sb[:, :], onesb_d), (tri_sb[:, :], tri_d),
                              (onesrow_sb[:, :], onesrow_d)):
                c.inc(S.p0_ld, eng.dma_start(dst, srcd), 16)
        else:
            c.inc(S.p0_ld, None, 16 * 5)
        if me == "gpsimd":
            src = AP(wproj_d.tensor, 0, [[E, 128], [128 * E, 8], [1, E]])
            c.inc(S.p0_ld, eng.dma_start(wproj_sb[:, :], src), 16)
            src = AP(wkqv_d.tensor, 0, [[3 * E, 128], [128 * 3 * E, 8], [1, 3 * E]])
            c.inc(S.p0_ld, eng.dma_start(kt_sb[:, 0:8 * 3072], src), 16)
            c.inc(S.p0_ld, eng.dma_start(bprojt_sb[:, :], AP(bproj_d.tensor, 0, [[E, 1], [1, E]])), 16)
        else:
            c.inc(S.p0_ld, None, 48)
        if me == "sync":
            c.wait(S.p0_ld)
            dst = AP(wkqv_scr.tensor, 0, [[3 * E, 128], [128 * 3 * E, 8], [1, 3 * E]])
            c.inc(S.p0_st, eng.dma_start(dst, kt_sb[:, 0:8 * 3072]), 16)
        else:
            c.inc(S.p0_st, None, 16)
        if me == "vector":
            eng.memset(cst_sb[:, 0:1], EPS)
            i = eng.memset(cst_sb[:, 1:2], EXPB)
            c.inc(S.p0_v, i)
        else:
            c.inc(S.p0_v, None)
        if me == "sync":
            c.wait(S.p0_st)

    run_phase(p0)

    def p1(c):
        me, eng = c.role, c.eng
        st = type("st", (), {})()
        st.xn_free = [0, 0]
        st.stageC_free = 0
        st.trev_count = {}
        st.vev_prev = 0
        st.kqst_prev = 0
        st.kst_cnt = 0
        st.ktr_cnt = 0
        st.xntr = {}
        st.kqtr_prev = 0
        st.pstr_free = 0
        st.expfree = [0, 0]
        st.ptfree = [0, 0]
        st.scount = [0] * 64
        st.nrmfree = [0, 0]
        st.rsfree = [0, 0]
        st.revac_done = 0
        st.prj_all_prev = 0
        st.x1st_free = 0
        st.x1ev_free = [0, 0]
        st.pending_R = None

        def prep_io(g):
            buf = g % 2
            if me == "scalar":
                c.wait(S.a_xn, st.xn_free[buf])
                if g % 4 < 2:
                    c.wait(S.c_x1, st.x1ev_free[buf] if g % 4 == 0 else st.stageC_free)
                i = eng.dma_start(xln_sb[:, buf * E:(buf + 1) * E],
                                  x_d[g * 128:(g + 1) * 128, :])
                c.inc(S.a_x, i, 16)
            else:
                c.inc(S.a_x, None, 16)
            for k in range(8):
                role = "sync" if k % 2 == 0 else "gpsimd"
                if me == role:
                    w = 8 * g + k - 2 if k >= 3 else [8 * g - 1, 8 * g, 8 * g - 2][k]
                    c.wait(S.a_tslab, max(0, w))
                    if g == 0 and k < 3:
                        c.wait(S.p0_st, 16)
                    i = eng.dma_start(slab_sb[:, (k % 3) * 3072:(k % 3 + 1) * 3072],
                                      wkqv_scr[k * 128:(k + 1) * 128, :])
                    c.inc(S.a_slab if k % 2 == 0 else S.a_slab2, i, 16)
                else:
                    c.inc(S.a_slab if k % 2 == 0 else S.a_slab2, None, 16)

        def prep_ln(g):
            buf = g % 2
            lb = (g % 2) * 8
            if me == "vector":
                c.wait(S.a_x, 16 * (g + 1))
                if g % 4 == 0:
                    c.wait(S.b_pv)
                xsrc = xln_sb[:, buf * E:(buf + 1) * E]
                i = eng.reduce_sum(ln_sb[:, lb:lb + 1], xsrc, axis=AX.X, negate=True)
                c.inc(S.a_lnv, i)
                i = eng.scalar_tensor_tensor(pt_sb[:, 0:1024], xsrc, 1.0, xsrc,
                                             ALU.mult, ALU.mult, accum_out=ln_sb[:, lb + 6:lb + 7])
                c.inc(S.a_lnv, i)
                c.wait(S.a_lnv)
                i = eng.tensor_scalar(ln_sb[:, lb + 1:lb + 2], ln_sb[:, lb:lb + 1],
                                      1.0 / E, None, ALU.mult)
                c.inc(S.a_lnv, i)
                c.wait(S.a_lnv)
                i = eng.tensor_tensor(ln_sb[:, lb + 2:lb + 3], ln_sb[:, lb + 1:lb + 2],
                                      ln_sb[:, lb + 1:lb + 2], ALU.mult)
                c.inc(S.a_lnv, i)
                c.wait(S.a_lnv)
                i = eng.tensor_scalar(ln_sb[:, lb + 3:lb + 4], ln_sb[:, lb + 6:lb + 7],
                                      1.0 / E, ln_sb[:, lb + 2:lb + 3], ALU.mult, ALU.subtract)
                c.inc(S.a_lnv, i)
            else:
                c.inc(S.a_lnv, None, 5)
            if me == "scalar":
                c.wait(S.a_lnv)
                i = eng.activation(ln_sb[:, lb + 4:lb + 5], ln_sb[:, lb + 3:lb + 4],
                                   AF.Ln, bias=cst_sb[:, 0:1])
                c.inc(S.a_lns, i)
                c.wait(S.a_lns)
                i = eng.activation(ln_sb[:, lb + 5:lb + 6], ln_sb[:, lb + 4:lb + 5],
                                   AF.Exp, scale=-0.5)
                c.inc(S.a_lns, i)
            else:
                c.inc(S.a_lns, None, 2)
            if me == "vector":
                c.wait(S.a_lns)
                c.wait(S.a_tr, st.xntr.get(g - 1, 0))
                i = eng.tensor_scalar(xn_sb[:, :], xln_sb[:, buf * E:(buf + 1) * E],
                                      ln_sb[:, lb + 1:lb + 2], ln_sb[:, lb + 5:lb + 6],
                                      ALU.add, ALU.mult)
                c.inc(S.a_xn, i)
            else:
                c.inc(S.a_xn, None)
            st.xn_free[buf] = S.a_xn.n

        def prep_tr(g):
            if me == "tensor":
                c.wait(S.a_xn, g + 1)
                c.wait(S.a_trev, st.pstr_free)
                i = None
                for ch in range(8):
                    i = eng.transpose(ps_tr[:, ch * 128:(ch + 1) * 128],
                                      xn_sb[:, ch * 128:(ch + 1) * 128], ident_sb[:, :])
                c.inc(S.a_tr, i)
            else:
                c.inc(S.a_tr, None)
            st.xntr[g] = S.a_tr.n
            if me == "vector":
                c.wait(S.a_tr, st.xntr[g])
                c.wait(S.a_tslab, 8 * g)
                c.inc(S.a_trev, eng.tensor_copy(xnt_sb[:, :], ps_tr[:, :]))
            else:
                c.inc(S.a_trev, None)
            st.trev_count[g] = S.a_trev.n
            st.pstr_free = S.a_trev.n

        def heavy(g):
            qt, s4 = divmod(g, 4)
            if me == "tensor":
                c.wait(S.a_trev, st.trev_count[g])
                c.wait(S.a_vev, st.vev_prev)
                c.wait(S.a_kqst, st.kqst_prev)
                if s4 == 0 and qt > 0:
                    c.wait(S.c_x1, st.stageC_free)
                    c.wait(S.b_exp)
                    c.wait(S.b_nrm)
            for k in range(8):
                if me == "tensor":
                    if k % 2 == 0:
                        c.wait(S.a_slab, 16 * (4 * g + k // 2 + 1))
                    else:
                        c.wait(S.a_slab2, 16 * (4 * g + (k + 1) // 2))
                    i = None
                    for n in range(6):
                        i = eng.matmul(ps_main[:, n * 512:(n + 1) * 512],
                                       xnt_sb[:, k * 128:(k + 1) * 128],
                                       slab_sb[:, (k % 3) * 3072 + n * 512:(k % 3) * 3072 + (n + 1) * 512],
                                       start=(k == 0), stop=(k == 7))
                    c.inc(S.a_tslab, i)
                else:
                    c.inc(S.a_tslab, None)
            if me == "scalar":
                c.wait(S.a_tslab, 8 * (g + 1))
                c.wait(S.a_tr, st.kqtr_prev)
                i = eng.activation(kq_sb[:, :], ps_main[:, 0:1024], AF.Copy)
                c.inc(S.a_kqst, i)
            else:
                c.inc(S.a_kqst, None)
            st.kst_cnt = S.a_kqst.n
            if me == "vector":
                c.wait(S.a_tslab, 8 * (g + 1))
                i = eng.tensor_copy(v_sb[:, g * E:(g + 1) * E], ps_main[:, 2048:3072])
                c.inc(S.a_vev, i)
            else:
                c.inc(S.a_vev, None)
            st.vev_prev = S.a_vev.n
            if (g + 1) % 4 != 0 and g + 1 < NSUB:
                prep_tr(g + 1)
            for which in ("K", "Q"):
                if which == "Q":
                    if me == "scalar":
                        c.wait(S.a_tr, st.ktr_cnt)
                        i = eng.activation(kq_sb[:, :], ps_main[:, 1024:2048], AF.Copy)
                        c.inc(S.a_kqst, i)
                    else:
                        c.inc(S.a_kqst, None)
                if me == "tensor":
                    c.wait(S.a_kqst, st.kst_cnt if which == "K" else S.a_kqst.n)
                    c.wait(S.a_trev, st.pstr_free)
                    i = None
                    for ch in range(8):
                        i = eng.transpose(ps_tr[:, ch * 128:(ch + 1) * 128],
                                          kq_sb[:, ch * 128:(ch + 1) * 128], ident_sb[:, :])
                    c.inc(S.a_tr, i)
                else:
                    c.inc(S.a_tr, None)
                if which == "K":
                    st.ktr_cnt = S.a_tr.n
                if me == "vector":
                    c.wait(S.a_tr)
                    if which == "K":
                        dst = AP(kt_sb, g * 128, [[NP * T, 128], [T, 8], [1, 128]])
                    else:
                        dst = AP(qt_sb, s4 * 128, [[NP * QTW, 128], [QTW, 8], [1, 128]])
                    srcp = AP(ps_tr, 0, [[1024, 128], [128, 8], [1, 128]])
                    c.inc(S.a_trev, eng.tensor_copy(dst, srcp))
                else:
                    c.inc(S.a_trev, None)
                st.pstr_free = S.a_trev.n
            st.kqtr_prev = S.a_tr.n
            st.kqst_prev = S.a_kqst.n

        def emit_R(pp, recips_done):
            par = pp % 2
            pvoff = 2048 + par * 512
            roff = 2048 + ((pp + 1) % 2) * 512
            rslanes = (0, 32) if par == 0 else (64, 96)
            if me == "tensor":
                c.wait(S.b_nrm, recips_done)
                c.wait(S.b_nrm, st.nrmfree[(pp + 1) % 2])
                eng.matmul(ps_main[0:64, roff:roff + 512],
                           ones_sb[rslanes[0]:rslanes[0] + 1, 0:64],
                           rr_sb[rslanes[0]:rslanes[0] + 1, par * 512:par * 512 + 512],
                           start=True, stop=True, tile_position=(rslanes[0], 0))
                i = eng.matmul(ps_main[64:128, roff:roff + 512],
                               ones_sb[rslanes[1]:rslanes[1] + 1, 0:64],
                               rr_sb[rslanes[1]:rslanes[1] + 1, par * 512:par * 512 + 512],
                               start=True, stop=True, tile_position=(rslanes[1], 64))
                c.inc(S.b_rmm, i)
            else:
                c.inc(S.b_rmm, None)
            if me == "vector":
                c.wait(S.b_rmm)
                if pp == 0:
                    c.wait(S.c_prj, st.prj_all_prev)
                eng.tensor_copy(rr_sb[0:64, par * 512:par * 512 + 512],
                                ps_main[0:64, roff:roff + 512])
                i = eng.tensor_copy(rr_sb[64:128, par * 512:par * 512 + 512],
                                    ps_main[64:128, roff:roff + 512])
                c.inc(S.b_nrm, i)
            else:
                c.inc(S.b_nrm, None)
            st.revac_done = S.b_nrm.n
            if me == "vector":
                eng.tensor_tensor(att_sb[0:64, pp * 512:(pp + 1) * 512],
                                  ps_main[0:64, pvoff:pvoff + 512],
                                  rr_sb[0:64, par * 512:par * 512 + 512], ALU.mult)
                i = eng.tensor_tensor(att_sb[64:128, pp * 512:(pp + 1) * 512],
                                      ps_main[64:128, pvoff:pvoff + 512],
                                      rr_sb[64:128, par * 512:par * 512 + 512], ALU.mult)
                c.inc(S.b_nrm, i)
            else:
                c.inc(S.b_nrm, None)
            st.nrmfree[par] = S.b_nrm.n

        def stage_b(qt):
            jlast = 4 * qt + 3
            for pp in range(NP):
                par = pp % 2
                pvoff = 2048 + par * 512
                rslanes = (0, 32) if par == 0 else (64, 96)
                ecount = {}
                mcount = {}
                pair_pv_base = S.b_pv.n

                def emit_S(j):
                    sbi = j % 2
                    soff = sbi * 1024
                    if me == "tensor":
                        if pp == 0 and j == 0:
                            c.wait(S.a_vev)
                            c.wait(S.a_trev)
                        c.wait(S.b_exp, st.expfree[sbi])
                        i = None
                        for half, base in ((0, 0), (1, 64)):
                            i = eng.matmul(ps_main[:, soff + half * 512: soff + (half + 1) * 512],
                                           kt_sb[base:base + 64, pp * T + j * 128: pp * T + (j + 1) * 128],
                                           qt_sb[base:base + 64, pp * QTW:(pp + 1) * QTW],
                                           start=True, stop=True)
                        c.inc(S.b_s, i)
                    else:
                        c.inc(S.b_s, None)
                    st.scount[j % 64] = S.b_s.n
                    if me == "scalar":
                        c.wait(S.b_s, st.scount[j % 64])
                        c.wait(S.b_pv, pair_pv_base + j - 1)
                        i = eng.activation(pt_sb[:, sbi * 1024:(sbi + 1) * 1024],
                                           ps_main[:, soff:soff + 1024], AF.Exp,
                                           bias=cst_sb[:, 1:2], scale=0.125)
                        c.inc(S.b_exp, i)
                    else:
                        c.inc(S.b_exp, None)
                    st.expfree[sbi] = S.b_exp.n
                    ecount[j] = S.b_exp.n
                    if j >= 4 * qt:
                        cz = (j - 4 * qt) * 128
                        if me == "vector":
                            c.wait(S.b_exp, ecount[j])
                            i = None
                            for half in range(2):
                                base = sbi * 1024 + half * 512
                                if cz > 0:
                                    eng.memset(pt_sb[:, base:base + cz], 0.0)
                                i = eng.tensor_tensor(pt_sb[:, base + cz:base + cz + 128],
                                                      pt_sb[:, base + cz:base + cz + 128],
                                                      tri_sb[:, :], ALU.mult)
                            c.inc(S.b_msk, i)
                        else:
                            c.inc(S.b_msk, None)
                        mcount[j] = S.b_msk.n
                    else:
                        mcount[j] = None

                def emit_PV(j):
                    sbi = j % 2
                    ptl = pt_sb[:, sbi * 1024:sbi * 1024 + 512]
                    pth = pt_sb[:, sbi * 1024 + 512:sbi * 1024 + 1024]
                    if me == "tensor":
                        if mcount[j] is not None:
                            c.wait(S.b_msk, mcount[j])
                        else:
                            c.wait(S.b_exp, ecount[j])
                        if j == 0:
                            c.wait(S.b_nrm, st.nrmfree[par])
                            c.wait(S.b_nrm, st.revac_done)
                            c.wait(S.b_nrm, st.rsfree[par])
                        stt = (j == 0)
                        sp = (j == jlast)
                        eng.matmul(ps_main[0:64, pvoff:pvoff + 512],
                                   v_sb[:, j * E + (2 * pp) * 64: j * E + (2 * pp) * 64 + 64],
                                   ptl, start=stt, stop=sp)
                        eng.matmul(ps_main[64:128, pvoff:pvoff + 512],
                                   v_sb[:, j * E + (2 * pp + 1) * 64: j * E + (2 * pp + 1) * 64 + 64],
                                   pth, start=stt, stop=sp, tile_position=(0, 64))
                        eng.matmul(ps_main[rslanes[0]:rslanes[0] + 1, 3072:3584], onesb_sb[:, 0:1],
                                   ptl, start=stt, stop=sp, tile_position=(0, rslanes[0]))
                        i = eng.matmul(ps_main[rslanes[1]:rslanes[1] + 1, 3072:3584], onesb_sb[:, 0:1],
                                       pth, start=stt, stop=sp, tile_position=(0, rslanes[1]))
                        c.inc(S.b_pv, i)
                    else:
                        c.inc(S.b_pv, None)
                    st.ptfree[sbi] = S.b_pv.n

                DELAY = 2
                for j in range(jlast + 1):
                    emit_S(j)
                    if j == 1 and st.pending_R is not None:
                        emit_R(*st.pending_R)
                        st.pending_R = None
                    if j - DELAY >= 0:
                        emit_PV(j - DELAY)
                if st.pending_R is not None:
                    emit_R(*st.pending_R)
                    st.pending_R = None
                for j in range(max(0, jlast + 1 - DELAY), jlast + 1):
                    emit_PV(j)
                if me == "scalar":
                    c.wait(S.b_pv, pair_pv_base + jlast + 1)
                    i = eng.activation(rr_sb[rslanes[0]:rslanes[0] + 1, par * 512:par * 512 + 512],
                                       ps_main[rslanes[0]:rslanes[0] + 1, 3072:3584], AF.Ln)
                    c.inc(S.b_nrm, i)
                    c.wait(S.b_nrm)
                    i = eng.activation(rr_sb[rslanes[0]:rslanes[0] + 1, par * 512:par * 512 + 512],
                                       rr_sb[rslanes[0]:rslanes[0] + 1, par * 512:par * 512 + 512],
                                       AF.Exp, scale=-1.0)
                    c.inc(S.b_nrm, i)
                else:
                    c.inc(S.b_nrm, None, 2)
                if me == "vector":
                    c.wait(S.b_pv, pair_pv_base + jlast + 1)
                    i = eng.reciprocal(rr_sb[rslanes[1]:rslanes[1] + 1, par * 512:par * 512 + 512],
                                       ps_main[rslanes[1]:rslanes[1] + 1, 3072:3584])
                    c.inc(S.b_nrm, i)
                else:
                    c.inc(S.b_nrm, None)
                st.rsfree[par] = S.b_nrm.n
                st.pending_R = (pp, S.b_nrm.n)
            if st.pending_R is not None:
                emit_R(*st.pending_R)
                st.pending_R = None

        def stage_c(qt, bnd_io=None):
            for t in range(4):
                buf = t % 2
                if me == "sync":
                    c.wait(S.a_xn)
                    c.wait(S.c_x1, st.x1ev_free[buf])
                    i = eng.dma_start(xln_sb[:, buf * E:(buf + 1) * E],
                                      x_d[(qt * 4 + t) * 128:(qt * 4 + t + 1) * 128, :])
                    c.inc(S.c_xr, i, 16)
                else:
                    c.inc(S.c_xr, None, 16)
                if me == "tensor":
                    if t == 0:
                        c.wait(S.b_nrm)
                        c.wait(S.b_exp)
                    c.wait(S.c_x1, st.x1ev_free[buf])
                    poff = buf * 1024
                    i = None
                    for k in range(8):
                        for h in range(2):
                            i = eng.matmul(ps_main[:, poff + h * 512:poff + (h + 1) * 512],
                                           att_sb[:, k * 512 + t * 128:k * 512 + t * 128 + 128],
                                           wproj_sb[:, k * E + h * 512:k * E + (h + 1) * 512],
                                           start=(k == 0), stop=False)
                    for h in range(2):
                        i = eng.matmul(ps_main[:, poff + h * 512:poff + (h + 1) * 512],
                                       onesrow_sb[0:1, :],
                                       bprojt_sb[0:1, h * 512:(h + 1) * 512],
                                       start=False, stop=True)
                    c.inc(S.c_prj, i)
                else:
                    c.inc(S.c_prj, None)
                if me == "vector":
                    c.wait(S.c_prj)
                    c.wait(S.c_xr)
                    c.wait(S.c_st, st.x1st_free)
                    poff = buf * 1024
                    i = eng.tensor_tensor(x1_sb[:, :], ps_main[:, poff:poff + 1024],
                                          xln_sb[:, buf * E:(buf + 1) * E], ALU.add)
                    c.inc(S.c_x1, i)
                else:
                    c.inc(S.c_x1, None)
                st.x1ev_free[buf] = S.c_x1.n
                if me == "gpsimd":
                    c.wait(S.c_x1)
                    i = eng.dma_start(x1_scr[(qt * 4 + t) * 128:(qt * 4 + t + 1) * 128, :],
                                      x1_sb[:, :])
                    c.inc(S.c_st, i, 16)
                else:
                    c.inc(S.c_st, None, 16)
                st.x1st_free = S.c_st.n
            st.stageC_free = S.c_x1.n
            st.prj_all_prev = S.c_prj.n
            st.xn_free = [S.a_xn.n, S.a_xn.n]

        prep_io(0)
        prep_ln(0)
        prep_tr(0)
        for qt in range(NQT):
            for s4 in range(4):
                g = qt * 4 + s4
                if g + 1 < NSUB and (g + 1) % 4 != 0:
                    prep_io(g + 1)
                    prep_ln(g + 1)
                heavy(g)
            stage_b(qt)
            stage_c(qt)
            if qt + 1 < NQT:
                g2 = 4 * (qt + 1)
                prep_io(g2)
                prep_ln(g2)
                prep_tr(g2)

    run_phase(p1)
    attn.close()

    mid = ExitStack()
    c_repl = sb(mid, "c_repl", [128, 10], F32)
    idx_sb = sb(mid, "idx_sb", [1, 2], mybir.dt.int32)
    cb2_repl = sb(mid, "cb2_repl", [128, E], F32)
    p23 = ExitStack()
    x1ln_sb = sb(p23, "x1ln_sb", [128, 3 * E], F32)
    ln2_sb = sb(p23, "ln2_sb", [128, 24], F32)
    xn1f_sb = sb(p23, "xn1f_sb", [128, 2 * E], F32)
    xn1b_sb = sb(p23, "xn1b_sb", [128, 2 * E], BF16)
    xn1t_sb = sb(p23, "xn1t_sb", [128, 2 * E], BF16)
    sqd_sb = sb(p23, "sqd_sb", [128, E], BF16)
    gatew_sb = sb(p23, "gatew_sb", [128, NE * 8], F32)
    seqrow_sb = sb(p23, "seqrow_sb", [1, E], F32)
    seqst_sb = sb(p23, "seqst_sb", [128, 8], F32)
    prob_sb = sb(p23, "prob_sb", [1, 80], F32)
    ccol_sb = sb(p23, "ccol_sb", [8, 1], F32)
    b2_sb = sb(p23, "b2_sb", [8, E], F32)
    cb2row_sb = sb(p23, "cb2row_sb", [1, E], F32)
    iota8_sb = sb(p23, "iota8_sb", [1, 8], F32)

    def p2(c):
        me, eng = c.role, c.eng
        if me == "sync":
            c.wait(S.c_st)
            c.inc(S.e_ld, eng.dma_start(gatew_sb[:, :],
                                        AP(gatew_d.tensor, 0, [[NE, 128], [128 * NE, 8], [1, NE]])), 16)
            c.inc(S.e_ld, eng.dma_start(b2_sb[:, :], b2_d), 16)
            c.inc(S.e_ld, eng.dma_start(iota8_sb[:, :], iota8_d), 16)
        else:
            c.inc(S.e_ld, None, 48)
        st2 = type("st2", (), {})()
        st2.xnf_free = [0, 0, 0]
        st2.seq_prev = 0
        st2.seq_prev2 = 0
        st2.tr_prev = 0
        st2.tr_prev2 = 0
        st2.st_prev = 0
        st2.st_prev2 = 0
        st2.lnv = {}
        st2.lns = {}

        def pre(g):
            buf = g % 3
            lb = buf * 8
            if me == "sync":
                c.wait(S.d_xnf, st2.xnf_free[buf])
                i = eng.dma_start(x1ln_sb[:, buf * E:(buf + 1) * E],
                                  x1_scr[g * 128:(g + 1) * 128, :])
                c.inc(S.d_x, i, 16)
            else:
                c.inc(S.d_x, None, 16)
            if me == "vector":
                c.wait(S.d_x, 16 * (g + 1))
                xsrc = x1ln_sb[:, buf * E:(buf + 1) * E]
                i = eng.reduce_sum(ln2_sb[:, lb:lb + 1], xsrc, axis=AX.X, negate=True)
                c.inc(S.d_lnv, i)
                i = eng.scalar_tensor_tensor(sqd_sb[:, :], xsrc, 1.0, xsrc,
                                             ALU.mult, ALU.mult, accum_out=ln2_sb[:, lb + 6:lb + 7])
                c.inc(S.d_lnv, i)
                c.wait(S.d_lnv)
                i = eng.tensor_scalar(ln2_sb[:, lb + 1:lb + 2], ln2_sb[:, lb:lb + 1],
                                      1.0 / E, None, ALU.mult)
                c.inc(S.d_lnv, i)
                c.wait(S.d_lnv)
                i = eng.tensor_tensor(ln2_sb[:, lb + 2:lb + 3], ln2_sb[:, lb + 1:lb + 2],
                                      ln2_sb[:, lb + 1:lb + 2], ALU.mult)
                c.inc(S.d_lnv, i)
                c.wait(S.d_lnv)
                i = eng.tensor_scalar(ln2_sb[:, lb + 3:lb + 4], ln2_sb[:, lb + 6:lb + 7],
                                      1.0 / E, ln2_sb[:, lb + 2:lb + 3], ALU.mult, ALU.subtract)
                c.inc(S.d_lnv, i)
            else:
                c.inc(S.d_lnv, None, 5)
            st2.lnv[g] = S.d_lnv.n
            if me == "scalar":
                c.wait(S.d_lnv, st2.lnv[g])
                i = eng.activation(ln2_sb[:, lb + 4:lb + 5], ln2_sb[:, lb + 3:lb + 4],
                                   AF.Ln, bias=cst_sb[:, 0:1])
                c.inc(S.d_lns, i)
                c.wait(S.d_lns)
                i = eng.activation(ln2_sb[:, lb + 5:lb + 6], ln2_sb[:, lb + 4:lb + 5],
                                   AF.Exp, scale=-0.5)
                c.inc(S.d_lns, i)
            else:
                c.inc(S.d_lns, None, 2)
            st2.lns[g] = S.d_lns.n

        def post(g):
            buf = g % 3
            lb = buf * 8
            fb = (g % 2) * E
            xsrc = x1ln_sb[:, buf * E:(buf + 1) * E]
            if me == "vector":
                c.wait(S.d_lns, st2.lns[g])
                c.wait(S.d_sum, st2.seq_prev2)
                i = eng.tensor_scalar(xn1f_sb[:, fb:fb + E], xsrc, ln2_sb[:, lb + 1:lb + 2],
                                      ln2_sb[:, lb + 5:lb + 6], ALU.add, ALU.mult)
                c.inc(S.d_xnf, i)
            else:
                c.inc(S.d_xnf, None)
            st2.xnf_free[buf] = S.d_xnf.n
            if me == "scalar":
                c.wait(S.d_xnf, st2.xnf_free[buf])
                c.wait(S.d_tr, st2.tr_prev2)
                i = eng.activation(xn1b_sb[:, fb:fb + E], xn1f_sb[:, fb:fb + E], AF.Copy)
                c.inc(S.d_xnb, i)
            else:
                c.inc(S.d_xnb, None)
            if me == "tensor":
                c.wait(S.d_xnf)
                i = None
                for h in range(2):
                    i = eng.matmul(ps_main[0:1, 2560 + h * 512:2560 + (h + 1) * 512],
                                   ones_sb[:, 0:1], xn1f_sb[:, fb + h * 512:fb + (h + 1) * 512],
                                   start=(g == 0), stop=(g == NSUB - 1))
                c.inc(S.d_sum, i)
            else:
                c.inc(S.d_sum, None)
            st2.seq_prev2 = st2.seq_prev
            st2.seq_prev = S.d_sum.n
            if me == "tensor":
                c.wait(S.d_xnb)
                c.wait(S.d_trev)
                i = None
                for ch in range(8):
                    i = eng.transpose(ps_tr[:, ch * 128:(ch + 1) * 128],
                                      xn1b_sb[:, fb + ch * 128:fb + (ch + 1) * 128], ident_sb[:, :])
                c.inc(S.d_tr, i)
            else:
                c.inc(S.d_tr, None)
            st2.tr_prev2 = st2.tr_prev
            st2.tr_prev = S.d_tr.n
            if me == "scalar":
                c.wait(S.d_tr, st2.tr_prev)
                c.wait(S.d_st, st2.st_prev2)
                c.inc(S.d_trev, eng.activation(xn1t_sb[:, fb:fb + E], ps_tr[:, :], AF.Copy))
            else:
                c.inc(S.d_trev, None)
            if me == "gpsimd":
                c.wait(S.d_trev, S.d_trev.n)
                dst = AP(xn1t_scr.tensor, g * 128, [[T, 128], [128 * T, 8], [1, 128]])
                c.inc(S.d_st, eng.dma_start(dst, xn1t_sb[:, fb:fb + E]), 16)
            else:
                c.inc(S.d_st, None, 16)
            st2.st_prev2 = st2.st_prev
            st2.st_prev = S.d_st.n

        pre(0)
        pre(1)
        for g in range(NSUB):
            post(g)
            if g + 2 < NSUB:
                pre(g + 2)

    run_phase(p2)

    def p3(c):
        me, eng = c.role, c.eng

        def vstep(fn):
            if me == "vector":
                c.inc(S.e_v, fn())
                c.wait(S.e_v)
            else:
                c.inc(S.e_v, None)

        def sstep(fn):
            if me == "scalar":
                c.inc(S.e_s, fn())
                c.wait(S.e_s)
            else:
                c.inc(S.e_s, None)

        if me == "vector":
            c.wait(S.d_sum)
        vstep(lambda: eng.tensor_copy(seqrow_sb[0:1, :], ps_main[0:1, 2560:3584]))
        if me == "sync":
            c.wait(S.e_v)
            c.inc(S.e_ld, eng.dma_start(vec_scr, seqrow_sb[0:1, :]), 16)
            c.wait(S.e_ld)
            c.inc(S.e_ld, eng.dma_start(seqst_sb[:, :], AP(vec_scr.tensor, 0, [[1, 128], [128, 8]]), allow_slow_non_contiguous=True), 16)
        else:
            c.inc(S.e_ld, None, 32)
        if me == "tensor":
            c.wait(S.e_ld)
            i = None
            for k in range(8):
                i = eng.matmul(ps_main[0:1, 0:8], seqst_sb[:, k:k + 1],
                               gatew_sb[:, k * 8:(k + 1) * 8], start=(k == 0), stop=(k == 7))
            c.inc(S.e_t, i)
        else:
            c.inc(S.e_t, None)
        if me == "scalar":
            c.wait(S.e_t)
        sstep(lambda: eng.activation(prob_sb[0:1, 0:8], ps_main[0:1, 0:8], AF.Copy, scale=1.0 / T))
        if me == "vector":
            c.wait(S.e_s)
        vstep(lambda: eng.reduce_max(prob_sb[0:1, 8:9], prob_sb[0:1, 0:8], axis=AX.X, negate=True))
        if me == "scalar":
            c.wait(S.e_v)
        sstep(lambda: eng.activation(prob_sb[0:1, 16:24], prob_sb[0:1, 0:8], AF.Exp,
                                     bias=prob_sb[0:1, 8:9]))
        if me == "vector":
            c.wait(S.e_s)
        vstep(lambda: eng.reduce_sum(prob_sb[0:1, 9:10], prob_sb[0:1, 16:24], axis=AX.X))
        vstep(lambda: eng.reciprocal(prob_sb[0:1, 10:11], prob_sb[0:1, 9:10]))
        vstep(lambda: eng.tensor_scalar(prob_sb[0:1, 24:32], prob_sb[0:1, 16:24],
                                        prob_sb[0:1, 10:11], None, ALU.mult))
        vstep(lambda: eng.reduce_max(prob_sb[0:1, 11:12], prob_sb[0:1, 24:32], axis=AX.X))
        vstep(lambda: eng.tensor_scalar(prob_sb[0:1, 32:40], prob_sb[0:1, 24:32],
                                        prob_sb[0:1, 11:12], None, ALU.is_equal))
        vstep(lambda: eng.tensor_tensor(prob_sb[0:1, 40:48], prob_sb[0:1, 24:32],
                                        prob_sb[0:1, 32:40], ALU.mult))
        vstep(lambda: eng.tensor_tensor(prob_sb[0:1, 48:56], prob_sb[0:1, 24:32],
                                        prob_sb[0:1, 40:48], ALU.subtract))
        vstep(lambda: eng.reduce_max(prob_sb[0:1, 12:13], prob_sb[0:1, 48:56], axis=AX.X))
        vstep(lambda: eng.tensor_scalar(prob_sb[0:1, 56:64], prob_sb[0:1, 48:56],
                                        prob_sb[0:1, 12:13], None, ALU.is_equal))
        vstep(lambda: eng.tensor_tensor(prob_sb[0:1, 13:14], prob_sb[0:1, 11:12],
                                        prob_sb[0:1, 12:13], ALU.add))
        vstep(lambda: eng.reciprocal(prob_sb[0:1, 14:15], prob_sb[0:1, 13:14]))
        vstep(lambda: eng.tensor_tensor(prob_sb[0:1, 40:48], prob_sb[0:1, 32:40],
                                        iota8_sb[0:1, :], ALU.mult))
        vstep(lambda: eng.reduce_sum(prob_sb[0:1, 74:75], prob_sb[0:1, 40:48], axis=AX.X))
        vstep(lambda: eng.tensor_tensor(prob_sb[0:1, 40:48], prob_sb[0:1, 56:64],
                                        iota8_sb[0:1, :], ALU.mult))
        vstep(lambda: eng.reduce_sum(prob_sb[0:1, 75:76], prob_sb[0:1, 40:48], axis=AX.X))
        vstep(lambda: eng.tensor_copy(idx_sb[0:1, :], prob_sb[0:1, 74:76]))
        vstep(lambda: eng.tensor_copy(prob_sb[0:1, 72:73], prob_sb[0:1, 11:12]))
        vstep(lambda: eng.tensor_copy(prob_sb[0:1, 73:74], prob_sb[0:1, 12:13]))
        vstep(lambda: eng.tensor_scalar(prob_sb[0:1, 72:74], prob_sb[0:1, 72:74],
                                        prob_sb[0:1, 14:15], None, ALU.mult))
        vstep(lambda: eng.tensor_scalar(prob_sb[0:1, 32:40], prob_sb[0:1, 32:40],
                                        prob_sb[0:1, 11:12], None, ALU.mult))
        vstep(lambda: eng.tensor_scalar(prob_sb[0:1, 56:64], prob_sb[0:1, 56:64],
                                        prob_sb[0:1, 12:13], None, ALU.mult))
        vstep(lambda: eng.tensor_tensor(prob_sb[0:1, 64:72], prob_sb[0:1, 32:40],
                                        prob_sb[0:1, 56:64], ALU.add))
        vstep(lambda: eng.tensor_scalar(prob_sb[0:1, 64:72], prob_sb[0:1, 64:72],
                                        prob_sb[0:1, 14:15], None, ALU.mult))
        if me == "sync":
            c.wait(S.e_v)
            c.inc(S.e_ld, eng.dma_start(AP(vec_scr.tensor, 0, [[NE, 1], [1, NE]]),
                                        prob_sb[0:1, 64:72]), 16)
            c.wait(S.e_ld)
            c.inc(S.e_ld, eng.dma_start(ccol_sb[:, :], AP(vec_scr.tensor, 0, [[1, 8], [1, 1]]), allow_slow_non_contiguous=True), 16)
        else:
            c.inc(S.e_ld, None, 32)
        if me == "tensor":
            c.wait(S.e_ld)
            i = None
            for h in range(2):
                i = eng.matmul(ps_main[0:1, 1024 + h * 512:1024 + (h + 1) * 512],
                               ccol_sb[:, 0:1], b2_sb[:, h * 512:(h + 1) * 512],
                               start=True, stop=True)
            c.inc(S.e_t, i)
        else:
            c.inc(S.e_t, None)
        if me == "vector":
            c.wait(S.e_t)
        vstep(lambda: eng.tensor_copy(cb2row_sb[0:1, :], ps_main[0:1, 1024:2048]))
        if me == "tensor":
            c.wait(S.e_v)
            for h in range(2):
                eng.matmul(ps_main[:, h * 512:(h + 1) * 512], ones_sb[0:1, 0:128],
                           cb2row_sb[0:1, h * 512:(h + 1) * 512], start=True, stop=True)
            i = eng.matmul(ps_main[:, 2048:2058], ones_sb[0:1, 0:128],
                           prob_sb[0:1, 64:74], start=True, stop=True)
            c.inc(S.e_t, i)
        else:
            c.inc(S.e_t, None)
        if me == "vector":
            c.wait(S.e_t)
        vstep(lambda: eng.tensor_copy(cb2_repl[:, :], ps_main[:, 0:1024]))
        vstep(lambda: eng.tensor_copy(c_repl[:, :], ps_main[:, 2048:2058]))
        if dbg_d is not None:
            if me == "sync":
                c.wait(S.e_v)
                c.inc(S.e_ld, eng.dma_start(dbg_d, prob_sb[0:1, :]), 16)
                c.wait(S.e_ld)
            else:
                c.inc(S.e_ld, None, 16)

    run_phase(p3)
    p23.close()

    ffn = ExitStack()
    w1_sb = sb(ffn, "w1_sb", [128, 8 * FF], BF16)
    w2_sb = sb(ffn, "w2_sb", [128, 32 * E], BF16)
    h1t_sb = sb(ffn, "h1t_sb", [128, 32 * TCH], BF16)
    xn1tc_sb = sb(ffn, "xn1tc_sb", [128, 2 * 8 * TCH], BF16)
    resid_sb = sb(ffn, "resid_sb", [128, 4 * E], F32)
    outb_sb = sb(ffn, "outb_sb", [128, 4 * E], F32)
    b1t_sb = sb(ffn, "b1t_sb", [128, 32], F32)

    NSLOT = 2

    def p4(c):
        from concourse.expressions import RuntimeValue
        me, eng = c.role, c.eng
        h1done_prev = 0
        ydone_prev = 0
        geldone_prev = 0
        xnc_free = [0, 0]
        res_free = [0, 0]
        outb_free = [0, 0]
        gel_free = [0, 0]
        yps_free = [0, 0]
        ych_prev = 0
        if me == "sync":
            c.wait(S.d_st)
        for e in range(NSLOT):
            src_scr = x1_scr if e == 0 else part_scr[(e - 1) % 2]
            dst_scr = out_d if e == NSLOT - 1 else part_scr[e % 2]
            if me == "gpsimd":
                regs = eng.alloc_register(f"p4w{e}")
                regb = eng.alloc_register(f"p4b{e}")
                eng.reg_load(regs, idx_sb[0:1, e:e + 1])
                eng.reg_mov(regb, regs)
                eng.reg_alu(regs, regs, E * FF, ALU.mult)
                eng.reg_alu(regb, regb, FF, ALU.mult)
                offw = RuntimeValue(regs, 0, (NE - 1) * E * FF)
                offb = RuntimeValue(regb, 0, (NE - 1) * FF)
                c.wait(S.f_h1, h1done_prev)
                src = AP(w1_d.tensor, offw, [[FF, 128], [128 * FF, 8], [1, FF]])
                c.inc(S.f_w, eng.dma_start(w1_sb[:, :], src), 16)
                c.wait(S.f_y, ydone_prev)
                src = AP(w2_d.tensor, offw, [[E, 128], [128 * E, 32], [1, E]])
                c.inc(S.f_w, eng.dma_start(w2_sb[:, :], src), 16)
                c.wait(S.f_gel, geldone_prev)
                src = AP(b1_d.tensor, offb, [[1, 128], [128, 32]])
                c.inc(S.f_w, eng.dma_start(b1t_sb[:, :], src, allow_slow_non_contiguous=True), 16)
            else:
                c.inc(S.f_w, None, 48)
            for ch in range(NCH):
                cbuf = ch % 2
                if me == "sync":
                    c.wait(S.f_h1, xnc_free[cbuf])
                    src = AP(xn1t_scr.tensor, ch * TCH, [[T, 128], [128 * T, 8], [1, TCH]])
                    c.inc(S.f_xn, eng.dma_start(xn1tc_sb[:, cbuf * 8 * TCH:(cbuf + 1) * 8 * TCH], src), 16)
                    c.wait(S.f_ev, res_free[cbuf])
                    if e > 0:
                        c.wait(S.f_st, 16 * ((e - 1) * NCH + ch + 1))
                    src = AP(src_scr.tensor, ch * TCH * E, [[E, 128], [128 * E, 2], [1, E]])
                    c.inc(S.f_res, eng.dma_start(resid_sb[:, cbuf * 2 * E:(cbuf + 1) * 2 * E], src), 16)
                else:
                    c.inc(S.f_xn, None, 16)
                    c.inc(S.f_res, None, 16)
                for m in range(32):
                    if me == "tensor":
                        if m == 0:
                            c.wait(S.f_w, 16 * (3 * e + 1))
                            c.wait(S.f_xn, 16 * (e * NCH + ch + 1))
                        c.wait(S.f_gel, gel_free[m % 2])
                        hoff = (m % 2) * 512
                        i = None
                        for k in range(8):
                            i = eng.matmul(ps_main[:, hoff:hoff + TCH],
                                           w1_sb[:, k * FF + m * 128:k * FF + (m + 1) * 128],
                                           xn1tc_sb[:, cbuf * 8 * TCH + k * TCH:cbuf * 8 * TCH + (k + 1) * TCH],
                                           start=(k == 0), stop=(k == 7))
                        c.inc(S.f_h1, i)
                    else:
                        c.inc(S.f_h1, None)
                    if me == "scalar":
                        c.wait(S.f_h1)
                        if m == 0:
                            c.wait(S.f_w, 16 * (3 * e + 3))
                            c.wait(S.f_y, ych_prev)
                        hoff = (m % 2) * 512
                        i = eng.activation(h1t_sb[:, m * TCH:(m + 1) * TCH],
                                           ps_main[:, hoff:hoff + TCH], AF.Gelu,
                                           bias=b1t_sb[:, m:m + 1])
                        c.inc(S.f_gel, i)
                    else:
                        c.inc(S.f_gel, None)
                    gel_free[m % 2] = S.f_gel.n
                xnc_free[cbuf] = S.f_h1.n
                if ch == NCH - 1:
                    h1done_prev = S.f_h1.n
                    geldone_prev = S.f_gel.n
                for t in range(2):
                    yoff = 1024 + t * 1024
                    if me == "tensor":
                        c.wait(S.f_gel)
                        if ch == 0 and t == 0:
                            c.wait(S.f_w, 16 * (3 * e + 2))
                        c.wait(S.f_ev, yps_free[t])
                        i = None
                        for k2 in range(32):
                            for h in range(2):
                                i = eng.matmul(ps_main[:, yoff + h * 512:yoff + (h + 1) * 512],
                                               h1t_sb[:, k2 * TCH + t * 128:k2 * TCH + t * 128 + 128],
                                               w2_sb[:, k2 * E + h * 512:k2 * E + (h + 1) * 512],
                                               start=(k2 == 0), stop=(k2 == 31))
                        c.inc(S.f_y, i)
                    else:
                        c.inc(S.f_y, None)
                    if me == "vector":
                        c.wait(S.f_y)
                        c.wait(S.f_res)
                        c.wait(S.f_st, outb_free[cbuf])
                        ob = outb_sb[:, cbuf * 2 * E + t * E:cbuf * 2 * E + (t + 1) * E]
                        i = eng.scalar_tensor_tensor(ob,
                                                     ps_main[:, yoff:yoff + 1024],
                                                     c_repl[:, 8 + e:9 + e],
                                                     resid_sb[:, cbuf * 2 * E + t * E:cbuf * 2 * E + (t + 1) * E],
                                                     ALU.mult, ALU.add)
                        if e == 0:
                            i = eng.tensor_tensor(ob, ob, cb2_repl[:, :], ALU.add)
                        c.inc(S.f_ev, i)
                    else:
                        c.inc(S.f_ev, None)
                    yps_free[t] = S.f_ev.n
                if ch == NCH - 1:
                    ydone_prev = S.f_y.n
                ych_prev = S.f_y.n
                res_free[cbuf] = S.f_ev.n
                if me == "gpsimd":
                    c.wait(S.f_ev)
                    dst = AP(dst_scr.tensor, ch * TCH * E, [[E, 128], [128 * E, 2], [1, E]])
                    c.inc(S.f_st, eng.dma_start(dst, outb_sb[:, cbuf * 2 * E:(cbuf + 1) * 2 * E]), 16)
                else:
                    c.inc(S.f_st, None, 16)
                outb_free[cbuf] = S.f_st.n
        if me == "sync":
            c.wait(S.f_st)

    run_phase(p4)
    ffn.close()
    mid.close()
    glob.close()
    ctxstack.close()
    return nc


_NC_CACHE = None


def kernel(L0_attn_x, L0_attn_W_kqv, L0_attn_W_proj, L0_attn_b_proj,
           L0_ffn_gate_w, L0_ffn_w1, L0_ffn_b1, L0_ffn_w2, L0_ffn_b2):
    global _NC_CACHE
    if _NC_CACHE is None:
        _NC_CACHE = build()
    nc = _NC_CACHE
    x = np.ascontiguousarray(np.asarray(L0_attn_x, dtype=np.float32))
    common = {
        "wkqv": np.ascontiguousarray(np.asarray(L0_attn_W_kqv, dtype=np.float32)),
        "wproj": np.ascontiguousarray(np.asarray(L0_attn_W_proj, dtype=np.float32)),
        "bproj": np.ascontiguousarray(np.asarray(L0_attn_b_proj, dtype=np.float32)),
        "gatew": np.ascontiguousarray(np.asarray(L0_ffn_gate_w, dtype=np.float32)),
        "w1": np.ascontiguousarray(np.asarray(L0_ffn_w1, dtype=np.float32)),
        "b1": np.ascontiguousarray(np.asarray(L0_ffn_b1, dtype=np.float32)),
        "w2": np.ascontiguousarray(np.asarray(L0_ffn_w2, dtype=np.float32)),
        "b2": np.ascontiguousarray(np.asarray(L0_ffn_b2, dtype=np.float32)),
    }
    in_maps = [dict(common, x=np.ascontiguousarray(x[b])) for b in range(8)]
    res = run_bass_kernel_spmd(nc, in_maps, core_ids=list(range(8)))
    return np.stack([res.results[b]["out"] for b in range(8)], axis=0)


# version 19
# speedup vs baseline: 3.1940x; 1.0181x over previous
import numpy as np
import ml_dtypes
from contextlib import ExitStack

import concourse.bass as bass
import concourse.mybir as mybir
from concourse.bass_utils import run_bass_kernel_spmd

F32 = mybir.dt.float32
BF16 = mybir.dt.bfloat16
AF = mybir.ActivationFunctionType
ALU = mybir.AluOpType
AX = mybir.AxisListType
AP = bass.AP

T, E, NH, HD = 4096, 1024, 16, 64
NP = NH // 2
QTW, NQT = 512, T // 512
NSUB = T // 128
FF, NE = 4096, 8
TCH, NCH = 256, T // 256
EPS = 1e-5
EXPB = -12.0


class Sem:
    __slots__ = ("h", "n")

    def __init__(self, h):
        self.h = h
        self.n = 0


class C:

    def __init__(self, role, eng, sems):
        self.role = role
        self.eng = eng
        self.sems = sems

    def inc(self, sem, instr, k=1):
        sem.n += k
        if instr is not None:
            instr.then_inc(sem.h, k)

    def wait(self, sem, n=None):
        v = sem.n if n is None else n
        if v > 0:
            self.eng.wait_ge(sem.h, v)


def build(debug=False):
    nc = bass.Bass("TRN2", target_bir_lowering=False)
    dbgkind = "ExternalOutput" if debug else "Internal"

    x_d = nc.dram_tensor("x", [T, E], F32, kind="ExternalInput").ap()
    wkqv_d = nc.dram_tensor("wkqv", [E, 3 * E], F32, kind="ExternalInput").ap()
    wproj_d = nc.dram_tensor("wproj", [E, E], F32, kind="ExternalInput").ap()
    bproj_d = nc.dram_tensor("bproj", [E], F32, kind="ExternalInput").ap()
    gatew_d = nc.dram_tensor("gatew", [E, NE], F32, kind="ExternalInput").ap()
    w1_d = nc.dram_tensor("w1", [NE, E, FF], F32, kind="ExternalInput").ap()
    b1_d = nc.dram_tensor("b1", [NE, FF], F32, kind="ExternalInput").ap()
    w2_d = nc.dram_tensor("w2", [NE, FF, E], F32, kind="ExternalInput").ap()
    b2_d = nc.dram_tensor("b2", [NE, E], F32, kind="ExternalInput").ap()
    out_d = nc.dram_tensor("out", [T, E], F32, kind="ExternalOutput").ap()
    dbg_d = nc.dram_tensor("dbg", [1, 80], F32, kind="ExternalOutput").ap() if debug else None
    if debug:
        dbg_kt = nc.dram_tensor("dbg_kt", [128, NP * T], BF16, kind="ExternalOutput").ap()
        dbg_v = nc.dram_tensor("dbg_v", [128, NSUB * E], BF16, kind="ExternalOutput").ap()
        dbg_qt = nc.dram_tensor("dbg_qt", [128, NP * QTW], BF16, kind="ExternalOutput").ap()
        dbg_att = nc.dram_tensor("dbg_att", [128, NP * QTW], BF16, kind="ExternalOutput").ap()

    wkqv_scr = nc.dram_tensor("wkqv_scr", [E, 3 * E], BF16, kind="Internal").ap()
    x1_scr = nc.dram_tensor("x1_scr", [T, E], F32, kind=dbgkind).ap()
    xn1t_scr = nc.dram_tensor("xn1t_scr", [E, T], BF16, kind=dbgkind).ap()
    part_scr = [nc.dram_tensor(f"part{i}", [T, E], F32, kind="Internal").ap() for i in range(2)]
    vec_scr = nc.dram_tensor("vec_scr", [E], F32, kind="Internal").ap()

    ident_d = nc.inline_tensor(np.asarray(np.eye(128, dtype=ml_dtypes.bfloat16)), name="identc").ap()
    ones_d = nc.inline_tensor(np.ones((128, 128), dtype=np.float32), name="onesc").ap()
    onesb_d = nc.inline_tensor(np.asarray(np.ones((128, 2), dtype=ml_dtypes.bfloat16)), name="onesbc").ap()
    uu = np.arange(128)[None, :]
    pp_ = np.arange(128)[:, None]
    tri_d = nc.inline_tensor(np.asarray((pp_ <= uu).astype(ml_dtypes.bfloat16)), name="tric").ap()
    onesrow_d = nc.inline_tensor(np.asarray(np.ones((1, 128), dtype=ml_dtypes.bfloat16)), name="onesrowc").ap()
    iota8_d = nc.inline_tensor(np.arange(8, dtype=np.float32)[None, :], name="iota8c").ap()

    sems = {}
    ctxstack = ExitStack()

    def mksem(name):
        s = Sem(ctxstack.enter_context(nc.semaphore(name)))
        sems[name] = s
        return s

    S = type("S", (), {})()
    for nm in ["p0_ld", "p0_t", "p0_v", "p0_st",
               "a_x", "a_slab", "a_slab2", "a_tslab", "a_lnv", "a_lns", "a_xn",
               "a_tr", "a_trev", "a_kqv", "a_kqst", "a_vev",
               "b_s", "b_exp", "b_msk", "b_pv", "b_nrm", "b_rmm",
               "c_xr", "c_prj", "c_x1", "c_st",
               "d_x", "d_lnv", "d_lns", "d_xnf", "d_xnb", "d_tr", "d_trev", "d_sum", "d_st",
               "e_ld", "e_t", "e_s", "e_v",
               "f_w", "f_w2", "f_xn", "f_res", "f_h1", "f_gel", "f_y", "f_ev", "f_st"]:
        setattr(S, nm, mksem(nm))

    glob = ExitStack()

    def sb(stack, name, shape, dt):
        return stack.enter_context(nc.sbuf_tensor(name, shape, dt))

    ident_sb = sb(glob, "ident_sb", [128, 128], BF16)
    ones_sb = sb(glob, "ones_sb", [128, 128], F32)
    onesb_sb = sb(glob, "onesb_sb", [128, 2], BF16)
    ln_sb = sb(glob, "ln_sb", [128, 16], F32)
    cst_sb = sb(glob, "cst_sb", [128, 2], F32)
    ps_main = glob.enter_context(nc.psum_tensor("ps_main", [128, 3584], F32))
    ps_tr = glob.enter_context(nc.psum_tensor("ps_tr", [128, 1024], BF16))

    def run_phase(fn):
        saved = {k: s.n for k, s in sems.items()}
        with nc.Block() as blk:
            for role, starter in (("sync", blk.sync), ("gpsimd", blk.gpsimd),
                                  ("tensor", blk.tensor), ("vector", blk.vector),
                                  ("scalar", blk.scalar)):
                for k, s in sems.items():
                    s.n = saved[k]

                def body(eng, role=role):
                    fn(C(role, eng, sems))

                starter(body)

    def ln_stats(c, src_ap, sqdump_ap, s_v, s_s):
        me, eng = c.role, c.eng
        if me == "vector":
            c.inc(s_v, eng.reduce_sum(ln_sb[:, 0:1], src_ap, axis=AX.X, negate=True))
            c.inc(s_v, eng.scalar_tensor_tensor(sqdump_ap, src_ap, 1.0, src_ap,
                                                ALU.mult, ALU.mult, accum_out=ln_sb[:, 6:7]))
        else:
            c.inc(s_v, None)
            c.inc(s_v, None)
        if me == "scalar":
            c.wait(s_v)
            c.inc(s_s, eng.activation(ln_sb[:, 1:2], ln_sb[:, 0:1], AF.Copy, scale=1.0 / E))
            c.wait(s_s)
            c.inc(s_s, eng.activation(ln_sb[:, 2:3], ln_sb[:, 1:2], AF.Square))
        else:
            c.inc(s_s, None)
            c.inc(s_s, None)
        if me == "vector":
            c.wait(s_s)
            c.inc(s_v, eng.tensor_scalar(ln_sb[:, 3:4], ln_sb[:, 6:7], 1.0 / E, ln_sb[:, 2:3],
                                         ALU.mult, ALU.subtract))
        else:
            c.inc(s_v, None)
        if me == "scalar":
            c.wait(s_v)
            c.inc(s_s, eng.activation(ln_sb[:, 4:5], ln_sb[:, 3:4], AF.Ln, bias=cst_sb[:, 0:1]))
            c.wait(s_s)
            c.inc(s_s, eng.activation(ln_sb[:, 5:6], ln_sb[:, 4:5], AF.Exp, scale=-0.5))
        else:
            c.inc(s_s, None)
            c.inc(s_s, None)

    attn = ExitStack()
    kt_sb = sb(attn, "kt_sb", [128, NP * T], BF16)
    v_sb = sb(attn, "v_sb", [128, NSUB * E], BF16)
    qt_sb = sb(attn, "qt_sb", [128, NP * QTW], BF16)
    slab_sb = sb(attn, "slab_sb", [128, 3 * 3072], BF16)
    wproj_sb = sb(attn, "wproj_sb", [128, 8 * E], BF16)
    tri_sb = sb(attn, "tri_sb", [128, 128], BF16)
    onesrow_sb = sb(attn, "onesrow_sb", [1, 128], BF16)
    bprojt_sb = sb(attn, "bprojt_sb", [1, E], BF16)
    xln_sb = sb(attn, "xln_sb", [128, 2 * E], F32)
    xn_sb = sb(attn, "xn_sb", [128, E], BF16)
    xnt_sb = sb(attn, "xnt_sb", [128, E], BF16)
    kq_sb = sb(attn, "kq_sb", [128, E], BF16)
    pt_sb = sb(attn, "pt_sb", [128, 2048], BF16)
    att_sb = sb(attn, "att_sb", [128, NP * QTW], BF16)
    rr_sb = sb(attn, "rr_sb", [128, 1024], F32)
    x1_sb = sb(attn, "x1_sb", [128, E], F32)

    def p0(c):
        me, eng = c.role, c.eng
        if me == "sync":
            for dst, srcd in ((ident_sb[:, :], ident_d), (ones_sb[:, :], ones_d),
                              (onesb_sb[:, :], onesb_d), (tri_sb[:, :], tri_d),
                              (onesrow_sb[:, :], onesrow_d)):
                c.inc(S.p0_ld, eng.dma_start(dst, srcd), 16)
        else:
            c.inc(S.p0_ld, None, 16 * 5)
        if me == "gpsimd":
            src = AP(wproj_d.tensor, 0, [[E, 128], [128 * E, 8], [1, E]])
            c.inc(S.p0_ld, eng.dma_start(wproj_sb[:, :], src), 16)
            src = AP(wkqv_d.tensor, 0, [[3 * E, 128], [128 * 3 * E, 8], [1, 3 * E]])
            c.inc(S.p0_ld, eng.dma_start(kt_sb[:, 0:8 * 3072], src), 16)
            c.inc(S.p0_ld, eng.dma_start(bprojt_sb[:, :], AP(bproj_d.tensor, 0, [[E, 1], [1, E]])), 16)
        else:
            c.inc(S.p0_ld, None, 48)
        if me == "sync":
            c.wait(S.p0_ld)
            dst = AP(wkqv_scr.tensor, 0, [[3 * E, 128], [128 * 3 * E, 8], [1, 3 * E]])
            c.inc(S.p0_st, eng.dma_start(dst, kt_sb[:, 0:8 * 3072]), 16)
        else:
            c.inc(S.p0_st, None, 16)
        if me == "vector":
            eng.memset(cst_sb[:, 0:1], EPS)
            i = eng.memset(cst_sb[:, 1:2], EXPB)
            c.inc(S.p0_v, i)
        else:
            c.inc(S.p0_v, None)
        if me == "sync":
            c.wait(S.p0_st)

    run_phase(p0)

    def p1(c):
        me, eng = c.role, c.eng
        st = type("st", (), {})()
        st.xn_free = [0, 0]
        st.stageC_free = 0
        st.trev_count = {}
        st.vev_prev = 0
        st.kqst_prev = 0
        st.kst_cnt = 0
        st.ktr_cnt = 0
        st.xntr = {}
        st.kqtr_prev = 0
        st.pstr_free = 0
        st.expfree = [0, 0]
        st.ptfree = [0, 0]
        st.scount = [0] * 64
        st.nrmfree = [0, 0]
        st.rsfree = [0, 0]
        st.revac_done = 0
        st.prj_all_prev = 0
        st.x1st_free = 0
        st.x1ev_free = [0, 0]
        st.pending_R = None

        def prep_io(g):
            buf = g % 2
            if me == "scalar":
                c.wait(S.a_xn, st.xn_free[buf])
                if g % 4 < 2:
                    c.wait(S.c_x1, st.x1ev_free[buf] if g % 4 == 0 else st.stageC_free)
                i = eng.dma_start(xln_sb[:, buf * E:(buf + 1) * E],
                                  x_d[g * 128:(g + 1) * 128, :])
                c.inc(S.a_x, i, 16)
            else:
                c.inc(S.a_x, None, 16)
            for k in range(8):
                role = "sync" if k % 2 == 0 else "gpsimd"
                if me == role:
                    w = 8 * g + k - 2 if k >= 3 else [8 * g - 1, 8 * g, 8 * g - 2][k]
                    c.wait(S.a_tslab, max(0, w))
                    if g == 0 and k < 3:
                        c.wait(S.p0_st, 16)
                    i = eng.dma_start(slab_sb[:, (k % 3) * 3072:(k % 3 + 1) * 3072],
                                      wkqv_scr[k * 128:(k + 1) * 128, :])
                    c.inc(S.a_slab if k % 2 == 0 else S.a_slab2, i, 16)
                else:
                    c.inc(S.a_slab if k % 2 == 0 else S.a_slab2, None, 16)

        def prep_ln(g):
            buf = g % 2
            lb = (g % 2) * 8
            if me == "vector":
                c.wait(S.a_x, 16 * (g + 1))
                if g % 4 == 0:
                    c.wait(S.b_pv)
                xsrc = xln_sb[:, buf * E:(buf + 1) * E]
                i = eng.reduce_sum(ln_sb[:, lb:lb + 1], xsrc, axis=AX.X, negate=True)
                c.inc(S.a_lnv, i)
                i = eng.scalar_tensor_tensor(pt_sb[:, 0:1024], xsrc, 1.0, xsrc,
                                             ALU.mult, ALU.mult, accum_out=ln_sb[:, lb + 6:lb + 7])
                c.inc(S.a_lnv, i)
                c.wait(S.a_lnv)
                i = eng.tensor_scalar(ln_sb[:, lb + 1:lb + 2], ln_sb[:, lb:lb + 1],
                                      1.0 / E, None, ALU.mult)
                c.inc(S.a_lnv, i)
                c.wait(S.a_lnv)
                i = eng.tensor_tensor(ln_sb[:, lb + 2:lb + 3], ln_sb[:, lb + 1:lb + 2],
                                      ln_sb[:, lb + 1:lb + 2], ALU.mult)
                c.inc(S.a_lnv, i)
                c.wait(S.a_lnv)
                i = eng.tensor_scalar(ln_sb[:, lb + 3:lb + 4], ln_sb[:, lb + 6:lb + 7],
                                      1.0 / E, ln_sb[:, lb + 2:lb + 3], ALU.mult, ALU.subtract)
                c.inc(S.a_lnv, i)
            else:
                c.inc(S.a_lnv, None, 5)
            if me == "scalar":
                c.wait(S.a_lnv)
                i = eng.activation(ln_sb[:, lb + 4:lb + 5], ln_sb[:, lb + 3:lb + 4],
                                   AF.Ln, bias=cst_sb[:, 0:1])
                c.inc(S.a_lns, i)
                c.wait(S.a_lns)
                i = eng.activation(ln_sb[:, lb + 5:lb + 6], ln_sb[:, lb + 4:lb + 5],
                                   AF.Exp, scale=-0.5)
                c.inc(S.a_lns, i)
            else:
                c.inc(S.a_lns, None, 2)
            if me == "vector":
                c.wait(S.a_lns)
                c.wait(S.a_tr, st.xntr.get(g - 1, 0))
                i = eng.tensor_scalar(xn_sb[:, :], xln_sb[:, buf * E:(buf + 1) * E],
                                      ln_sb[:, lb + 1:lb + 2], ln_sb[:, lb + 5:lb + 6],
                                      ALU.add, ALU.mult)
                c.inc(S.a_xn, i)
            else:
                c.inc(S.a_xn, None)
            st.xn_free[buf] = S.a_xn.n

        def prep_tr(g):
            if me == "tensor":
                c.wait(S.a_xn, g + 1)
                c.wait(S.a_trev, st.pstr_free)
                i = None
                for ch in range(8):
                    i = eng.transpose(ps_tr[:, ch * 128:(ch + 1) * 128],
                                      xn_sb[:, ch * 128:(ch + 1) * 128], ident_sb[:, :])
                c.inc(S.a_tr, i)
            else:
                c.inc(S.a_tr, None)
            st.xntr[g] = S.a_tr.n
            if me == "vector":
                c.wait(S.a_tr, st.xntr[g])
                c.wait(S.a_tslab, 8 * g)
                c.inc(S.a_trev, eng.tensor_copy(xnt_sb[:, :], ps_tr[:, :]))
            else:
                c.inc(S.a_trev, None)
            st.trev_count[g] = S.a_trev.n
            st.pstr_free = S.a_trev.n

        def heavy(g):
            qt, s4 = divmod(g, 4)
            if me == "tensor":
                c.wait(S.a_trev, st.trev_count[g])
                c.wait(S.a_vev, st.vev_prev)
                c.wait(S.a_kqst, st.kqst_prev)
                if s4 == 0 and qt > 0:
                    c.wait(S.c_x1, st.stageC_free)
                    c.wait(S.b_exp)
                    c.wait(S.b_nrm)
            for k in range(8):
                if me == "tensor":
                    if k % 2 == 0:
                        c.wait(S.a_slab, 16 * (4 * g + k // 2 + 1))
                    else:
                        c.wait(S.a_slab2, 16 * (4 * g + (k + 1) // 2))
                    i = None
                    for n in range(6):
                        i = eng.matmul(ps_main[:, n * 512:(n + 1) * 512],
                                       xnt_sb[:, k * 128:(k + 1) * 128],
                                       slab_sb[:, (k % 3) * 3072 + n * 512:(k % 3) * 3072 + (n + 1) * 512],
                                       start=(k == 0), stop=(k == 7))
                    c.inc(S.a_tslab, i)
                else:
                    c.inc(S.a_tslab, None)
            if me == "scalar":
                c.wait(S.a_tslab, 8 * (g + 1))
                c.wait(S.a_tr, st.kqtr_prev)
                i = eng.activation(kq_sb[:, :], ps_main[:, 0:1024], AF.Copy)
                c.inc(S.a_kqst, i)
            else:
                c.inc(S.a_kqst, None)
            st.kst_cnt = S.a_kqst.n
            if me == "vector":
                c.wait(S.a_tslab, 8 * (g + 1))
                i = eng.tensor_copy(v_sb[:, g * E:(g + 1) * E], ps_main[:, 2048:3072])
                c.inc(S.a_vev, i)
            else:
                c.inc(S.a_vev, None)
            st.vev_prev = S.a_vev.n
            if (g + 1) % 4 != 0 and g + 1 < NSUB:
                prep_tr(g + 1)
            for which in ("K", "Q"):
                if which == "Q":
                    if me == "scalar":
                        c.wait(S.a_tr, st.ktr_cnt)
                        i = eng.activation(kq_sb[:, :], ps_main[:, 1024:2048], AF.Copy)
                        c.inc(S.a_kqst, i)
                    else:
                        c.inc(S.a_kqst, None)
                if me == "tensor":
                    c.wait(S.a_kqst, st.kst_cnt if which == "K" else S.a_kqst.n)
                    c.wait(S.a_trev, st.pstr_free)
                    i = None
                    for ch in range(8):
                        i = eng.transpose(ps_tr[:, ch * 128:(ch + 1) * 128],
                                          kq_sb[:, ch * 128:(ch + 1) * 128], ident_sb[:, :])
                    c.inc(S.a_tr, i)
                else:
                    c.inc(S.a_tr, None)
                if which == "K":
                    st.ktr_cnt = S.a_tr.n
                if me == "vector":
                    c.wait(S.a_tr)
                    if which == "K":
                        dst = AP(kt_sb, g * 128, [[NP * T, 128], [T, 8], [1, 128]])
                    else:
                        dst = AP(qt_sb, s4 * 128, [[NP * QTW, 128], [QTW, 8], [1, 128]])
                    srcp = AP(ps_tr, 0, [[1024, 128], [128, 8], [1, 128]])
                    c.inc(S.a_trev, eng.tensor_copy(dst, srcp))
                else:
                    c.inc(S.a_trev, None)
                st.pstr_free = S.a_trev.n
            st.kqtr_prev = S.a_tr.n
            st.kqst_prev = S.a_kqst.n

        def emit_R(pp, recips_done):
            par = pp % 2
            pvoff = 2048 + par * 512
            roff = 2048 + ((pp + 1) % 2) * 512
            rslanes = (0, 32) if par == 0 else (64, 96)
            if me == "tensor":
                c.wait(S.b_nrm, recips_done)
                c.wait(S.b_nrm, st.nrmfree[(pp + 1) % 2])
                eng.matmul(ps_main[0:64, roff:roff + 512],
                           ones_sb[rslanes[0]:rslanes[0] + 1, 0:64],
                           rr_sb[rslanes[0]:rslanes[0] + 1, par * 512:par * 512 + 512],
                           start=True, stop=True, tile_position=(rslanes[0], 0))
                i = eng.matmul(ps_main[64:128, roff:roff + 512],
                               ones_sb[rslanes[1]:rslanes[1] + 1, 0:64],
                               rr_sb[rslanes[1]:rslanes[1] + 1, par * 512:par * 512 + 512],
                               start=True, stop=True, tile_position=(rslanes[1], 64))
                c.inc(S.b_rmm, i)
            else:
                c.inc(S.b_rmm, None)
            if me == "vector":
                c.wait(S.b_rmm)
                if pp == 0:
                    c.wait(S.c_prj, st.prj_all_prev)
                eng.tensor_copy(rr_sb[0:64, par * 512:par * 512 + 512],
                                ps_main[0:64, roff:roff + 512])
                i = eng.tensor_copy(rr_sb[64:128, par * 512:par * 512 + 512],
                                    ps_main[64:128, roff:roff + 512])
                c.inc(S.b_nrm, i)
            else:
                c.inc(S.b_nrm, None)
            st.revac_done = S.b_nrm.n
            if me == "vector":
                eng.tensor_tensor(att_sb[0:64, pp * 512:(pp + 1) * 512],
                                  ps_main[0:64, pvoff:pvoff + 512],
                                  rr_sb[0:64, par * 512:par * 512 + 512], ALU.mult)
                i = eng.tensor_tensor(att_sb[64:128, pp * 512:(pp + 1) * 512],
                                      ps_main[64:128, pvoff:pvoff + 512],
                                      rr_sb[64:128, par * 512:par * 512 + 512], ALU.mult)
                c.inc(S.b_nrm, i)
            else:
                c.inc(S.b_nrm, None)
            st.nrmfree[par] = S.b_nrm.n

        def stage_b(qt):
            jlast = 4 * qt + 3
            for pp in range(NP):
                par = pp % 2
                pvoff = 2048 + par * 512
                rslanes = (0, 32) if par == 0 else (64, 96)
                ecount = {}
                mcount = {}
                pair_pv_base = S.b_pv.n

                def emit_S(j):
                    sbi = j % 2
                    soff = sbi * 1024
                    if me == "tensor":
                        if pp == 0 and j == 0:
                            c.wait(S.a_vev)
                            c.wait(S.a_trev)
                        c.wait(S.b_exp, st.expfree[sbi])
                        i = None
                        for half, base in ((0, 0), (1, 64)):
                            i = eng.matmul(ps_main[:, soff + half * 512: soff + (half + 1) * 512],
                                           kt_sb[base:base + 64, pp * T + j * 128: pp * T + (j + 1) * 128],
                                           qt_sb[base:base + 64, pp * QTW:(pp + 1) * QTW],
                                           start=True, stop=True)
                        c.inc(S.b_s, i)
                    else:
                        c.inc(S.b_s, None)
                    st.scount[j % 64] = S.b_s.n
                    if me == "scalar":
                        c.wait(S.b_s, st.scount[j % 64])
                        c.wait(S.b_pv, pair_pv_base + j - 1)
                        i = eng.activation(pt_sb[:, sbi * 1024:(sbi + 1) * 1024],
                                           ps_main[:, soff:soff + 1024], AF.Exp,
                                           bias=cst_sb[:, 1:2], scale=0.125)
                        c.inc(S.b_exp, i)
                    else:
                        c.inc(S.b_exp, None)
                    st.expfree[sbi] = S.b_exp.n
                    ecount[j] = S.b_exp.n
                    if j >= 4 * qt:
                        cz = (j - 4 * qt) * 128
                        if me == "vector":
                            c.wait(S.b_exp, ecount[j])
                            i = None
                            for half in range(2):
                                base = sbi * 1024 + half * 512
                                if cz > 0:
                                    eng.memset(pt_sb[:, base:base + cz], 0.0)
                                i = eng.tensor_tensor(pt_sb[:, base + cz:base + cz + 128],
                                                      pt_sb[:, base + cz:base + cz + 128],
                                                      tri_sb[:, :], ALU.mult)
                            c.inc(S.b_msk, i)
                        else:
                            c.inc(S.b_msk, None)
                        mcount[j] = S.b_msk.n
                    else:
                        mcount[j] = None

                def emit_PV(j):
                    sbi = j % 2
                    ptl = pt_sb[:, sbi * 1024:sbi * 1024 + 512]
                    pth = pt_sb[:, sbi * 1024 + 512:sbi * 1024 + 1024]
                    if me == "tensor":
                        if mcount[j] is not None:
                            c.wait(S.b_msk, mcount[j])
                        else:
                            c.wait(S.b_exp, ecount[j])
                        if j == 0:
                            c.wait(S.b_nrm, st.nrmfree[par])
                            c.wait(S.b_nrm, st.revac_done)
                            c.wait(S.b_nrm, st.rsfree[par])
                        stt = (j == 0)
                        sp = (j == jlast)
                        eng.matmul(ps_main[0:64, pvoff:pvoff + 512],
                                   v_sb[:, j * E + (2 * pp) * 64: j * E + (2 * pp) * 64 + 64],
                                   ptl, start=stt, stop=sp)
                        eng.matmul(ps_main[64:128, pvoff:pvoff + 512],
                                   v_sb[:, j * E + (2 * pp + 1) * 64: j * E + (2 * pp + 1) * 64 + 64],
                                   pth, start=stt, stop=sp, tile_position=(0, 64))
                        eng.matmul(ps_main[rslanes[0]:rslanes[0] + 1, 3072:3584], onesb_sb[:, 0:1],
                                   ptl, start=stt, stop=sp, tile_position=(0, rslanes[0]))
                        i = eng.matmul(ps_main[rslanes[1]:rslanes[1] + 1, 3072:3584], onesb_sb[:, 0:1],
                                       pth, start=stt, stop=sp, tile_position=(0, rslanes[1]))
                        c.inc(S.b_pv, i)
                    else:
                        c.inc(S.b_pv, None)
                    st.ptfree[sbi] = S.b_pv.n

                DELAY = 2
                for j in range(jlast + 1):
                    emit_S(j)
                    if j == 1 and st.pending_R is not None:
                        emit_R(*st.pending_R)
                        st.pending_R = None
                    if j - DELAY >= 0:
                        emit_PV(j - DELAY)
                if st.pending_R is not None:
                    emit_R(*st.pending_R)
                    st.pending_R = None
                for j in range(max(0, jlast + 1 - DELAY), jlast + 1):
                    emit_PV(j)
                if me == "scalar":
                    c.wait(S.b_pv, pair_pv_base + jlast + 1)
                    i = eng.activation(rr_sb[rslanes[0]:rslanes[0] + 1, par * 512:par * 512 + 512],
                                       ps_main[rslanes[0]:rslanes[0] + 1, 3072:3584], AF.Ln)
                    c.inc(S.b_nrm, i)
                    c.wait(S.b_nrm)
                    i = eng.activation(rr_sb[rslanes[0]:rslanes[0] + 1, par * 512:par * 512 + 512],
                                       rr_sb[rslanes[0]:rslanes[0] + 1, par * 512:par * 512 + 512],
                                       AF.Exp, scale=-1.0)
                    c.inc(S.b_nrm, i)
                else:
                    c.inc(S.b_nrm, None, 2)
                if me == "vector":
                    c.wait(S.b_pv, pair_pv_base + jlast + 1)
                    i = eng.reciprocal(rr_sb[rslanes[1]:rslanes[1] + 1, par * 512:par * 512 + 512],
                                       ps_main[rslanes[1]:rslanes[1] + 1, 3072:3584])
                    c.inc(S.b_nrm, i)
                else:
                    c.inc(S.b_nrm, None)
                st.rsfree[par] = S.b_nrm.n
                st.pending_R = (pp, S.b_nrm.n)
            if st.pending_R is not None:
                emit_R(*st.pending_R)
                st.pending_R = None

        def stage_c(qt, bnd_io=None):
            for t in range(4):
                buf = t % 2
                if me == "sync":
                    c.wait(S.a_xn)
                    c.wait(S.c_x1, st.x1ev_free[buf])
                    i = eng.dma_start(xln_sb[:, buf * E:(buf + 1) * E],
                                      x_d[(qt * 4 + t) * 128:(qt * 4 + t + 1) * 128, :])
                    c.inc(S.c_xr, i, 16)
                else:
                    c.inc(S.c_xr, None, 16)
                if me == "tensor":
                    if t == 0:
                        c.wait(S.b_nrm)
                        c.wait(S.b_exp)
                    c.wait(S.c_x1, st.x1ev_free[buf])
                    poff = buf * 1024
                    i = None
                    for k in range(8):
                        for h in range(2):
                            i = eng.matmul(ps_main[:, poff + h * 512:poff + (h + 1) * 512],
                                           att_sb[:, k * 512 + t * 128:k * 512 + t * 128 + 128],
                                           wproj_sb[:, k * E + h * 512:k * E + (h + 1) * 512],
                                           start=(k == 0), stop=False)
                    for h in range(2):
                        i = eng.matmul(ps_main[:, poff + h * 512:poff + (h + 1) * 512],
                                       onesrow_sb[0:1, :],
                                       bprojt_sb[0:1, h * 512:(h + 1) * 512],
                                       start=False, stop=True)
                    c.inc(S.c_prj, i)
                else:
                    c.inc(S.c_prj, None)
                if me == "vector":
                    c.wait(S.c_prj)
                    c.wait(S.c_xr)
                    c.wait(S.c_st, st.x1st_free)
                    poff = buf * 1024
                    i = eng.tensor_tensor(x1_sb[:, :], ps_main[:, poff:poff + 1024],
                                          xln_sb[:, buf * E:(buf + 1) * E], ALU.add)
                    c.inc(S.c_x1, i)
                else:
                    c.inc(S.c_x1, None)
                st.x1ev_free[buf] = S.c_x1.n
                if me == "gpsimd":
                    c.wait(S.c_x1)
                    i = eng.dma_start(x1_scr[(qt * 4 + t) * 128:(qt * 4 + t + 1) * 128, :],
                                      x1_sb[:, :])
                    c.inc(S.c_st, i, 16)
                else:
                    c.inc(S.c_st, None, 16)
                st.x1st_free = S.c_st.n
            st.stageC_free = S.c_x1.n
            st.prj_all_prev = S.c_prj.n
            st.xn_free = [S.a_xn.n, S.a_xn.n]

        prep_io(0)
        prep_ln(0)
        prep_tr(0)
        for qt in range(NQT):
            for s4 in range(4):
                g = qt * 4 + s4
                if g + 1 < NSUB and (g + 1) % 4 != 0:
                    prep_io(g + 1)
                    prep_ln(g + 1)
                heavy(g)
            stage_b(qt)
            stage_c(qt)
            if qt + 1 < NQT:
                g2 = 4 * (qt + 1)
                prep_io(g2)
                prep_ln(g2)
                prep_tr(g2)

    run_phase(p1)
    attn.close()

    mid = ExitStack()
    c_repl = sb(mid, "c_repl", [128, 10], F32)
    idx_sb = sb(mid, "idx_sb", [1, 2], mybir.dt.int32)
    cb2_repl = sb(mid, "cb2_repl", [128, E], F32)
    p23 = ExitStack()
    x1ln_sb = sb(p23, "x1ln_sb", [128, 3 * E], F32)
    ln2_sb = sb(p23, "ln2_sb", [128, 24], F32)
    xn1f_sb = sb(p23, "xn1f_sb", [128, 2 * E], F32)
    xn1b_sb = sb(p23, "xn1b_sb", [128, 2 * E], BF16)
    xn1t_sb = sb(p23, "xn1t_sb", [128, 2 * E], BF16)
    sqd_sb = sb(p23, "sqd_sb", [128, E], BF16)
    gatew_sb = sb(p23, "gatew_sb", [128, NE * 8], F32)
    seqrow_sb = sb(p23, "seqrow_sb", [1, E], F32)
    seqst_sb = sb(p23, "seqst_sb", [128, 8], F32)
    prob_sb = sb(p23, "prob_sb", [1, 80], F32)
    ccol_sb = sb(p23, "ccol_sb", [8, 1], F32)
    b2_sb = sb(p23, "b2_sb", [8, E], F32)
    cb2row_sb = sb(p23, "cb2row_sb", [1, E], F32)
    iota8_sb = sb(p23, "iota8_sb", [1, 8], F32)

    def p2(c):
        me, eng = c.role, c.eng
        if me == "sync":
            c.wait(S.c_st)
            c.inc(S.e_ld, eng.dma_start(gatew_sb[:, :],
                                        AP(gatew_d.tensor, 0, [[NE, 128], [128 * NE, 8], [1, NE]])), 16)
            c.inc(S.e_ld, eng.dma_start(b2_sb[:, :], b2_d), 16)
            c.inc(S.e_ld, eng.dma_start(iota8_sb[:, :], iota8_d), 16)
        else:
            c.inc(S.e_ld, None, 48)
        st2 = type("st2", (), {})()
        st2.xnf_free = [0, 0, 0]
        st2.seq_prev = 0
        st2.seq_prev2 = 0
        st2.tr_prev = 0
        st2.tr_prev2 = 0
        st2.st_prev = 0
        st2.st_prev2 = 0
        st2.lnv = {}
        st2.lns = {}

        def pre(g):
            buf = g % 3
            lb = buf * 8
            if me == "sync":
                c.wait(S.d_xnf, st2.xnf_free[buf])
                i = eng.dma_start(x1ln_sb[:, buf * E:(buf + 1) * E],
                                  x1_scr[g * 128:(g + 1) * 128, :])
                c.inc(S.d_x, i, 16)
            else:
                c.inc(S.d_x, None, 16)
            if me == "vector":
                c.wait(S.d_x, 16 * (g + 1))
                xsrc = x1ln_sb[:, buf * E:(buf + 1) * E]
                i = eng.reduce_sum(ln2_sb[:, lb:lb + 1], xsrc, axis=AX.X, negate=True)
                c.inc(S.d_lnv, i)
                i = eng.scalar_tensor_tensor(sqd_sb[:, :], xsrc, 1.0, xsrc,
                                             ALU.mult, ALU.mult, accum_out=ln2_sb[:, lb + 6:lb + 7])
                c.inc(S.d_lnv, i)
                c.wait(S.d_lnv)
                i = eng.tensor_scalar(ln2_sb[:, lb + 1:lb + 2], ln2_sb[:, lb:lb + 1],
                                      1.0 / E, None, ALU.mult)
                c.inc(S.d_lnv, i)
                c.wait(S.d_lnv)
                i = eng.tensor_tensor(ln2_sb[:, lb + 2:lb + 3], ln2_sb[:, lb + 1:lb + 2],
                                      ln2_sb[:, lb + 1:lb + 2], ALU.mult)
                c.inc(S.d_lnv, i)
                c.wait(S.d_lnv)
                i = eng.tensor_scalar(ln2_sb[:, lb + 3:lb + 4], ln2_sb[:, lb + 6:lb + 7],
                                      1.0 / E, ln2_sb[:, lb + 2:lb + 3], ALU.mult, ALU.subtract)
                c.inc(S.d_lnv, i)
            else:
                c.inc(S.d_lnv, None, 5)
            st2.lnv[g] = S.d_lnv.n
            if me == "scalar":
                c.wait(S.d_lnv, st2.lnv[g])
                i = eng.activation(ln2_sb[:, lb + 4:lb + 5], ln2_sb[:, lb + 3:lb + 4],
                                   AF.Ln, bias=cst_sb[:, 0:1])
                c.inc(S.d_lns, i)
                c.wait(S.d_lns)
                i = eng.activation(ln2_sb[:, lb + 5:lb + 6], ln2_sb[:, lb + 4:lb + 5],
                                   AF.Exp, scale=-0.5)
                c.inc(S.d_lns, i)
            else:
                c.inc(S.d_lns, None, 2)
            st2.lns[g] = S.d_lns.n

        def post(g):
            buf = g % 3
            lb = buf * 8
            fb = (g % 2) * E
            xsrc = x1ln_sb[:, buf * E:(buf + 1) * E]
            if me == "vector":
                c.wait(S.d_lns, st2.lns[g])
                c.wait(S.d_sum, st2.seq_prev2)
                i = eng.tensor_scalar(xn1f_sb[:, fb:fb + E], xsrc, ln2_sb[:, lb + 1:lb + 2],
                                      ln2_sb[:, lb + 5:lb + 6], ALU.add, ALU.mult)
                c.inc(S.d_xnf, i)
            else:
                c.inc(S.d_xnf, None)
            st2.xnf_free[buf] = S.d_xnf.n
            if me == "scalar":
                c.wait(S.d_xnf, st2.xnf_free[buf])
                c.wait(S.d_tr, st2.tr_prev2)
                i = eng.activation(xn1b_sb[:, fb:fb + E], xn1f_sb[:, fb:fb + E], AF.Copy)
                c.inc(S.d_xnb, i)
            else:
                c.inc(S.d_xnb, None)
            if me == "tensor":
                c.wait(S.d_xnf)
                i = None
                for h in range(2):
                    i = eng.matmul(ps_main[0:1, 2560 + h * 512:2560 + (h + 1) * 512],
                                   ones_sb[:, 0:1], xn1f_sb[:, fb + h * 512:fb + (h + 1) * 512],
                                   start=(g == 0), stop=(g == NSUB - 1))
                c.inc(S.d_sum, i)
            else:
                c.inc(S.d_sum, None)
            st2.seq_prev2 = st2.seq_prev
            st2.seq_prev = S.d_sum.n
            if me == "tensor":
                c.wait(S.d_xnb)
                c.wait(S.d_trev)
                i = None
                for ch in range(8):
                    i = eng.transpose(ps_tr[:, ch * 128:(ch + 1) * 128],
                                      xn1b_sb[:, fb + ch * 128:fb + (ch + 1) * 128], ident_sb[:, :])
                c.inc(S.d_tr, i)
            else:
                c.inc(S.d_tr, None)
            st2.tr_prev2 = st2.tr_prev
            st2.tr_prev = S.d_tr.n
            if me == "scalar":
                c.wait(S.d_tr, st2.tr_prev)
                c.wait(S.d_st, st2.st_prev2)
                c.inc(S.d_trev, eng.activation(xn1t_sb[:, fb:fb + E], ps_tr[:, :], AF.Copy))
            else:
                c.inc(S.d_trev, None)
            if me == "gpsimd":
                c.wait(S.d_trev, S.d_trev.n)
                dst = AP(xn1t_scr.tensor, g * 128, [[T, 128], [128 * T, 8], [1, 128]])
                c.inc(S.d_st, eng.dma_start(dst, xn1t_sb[:, fb:fb + E]), 16)
            else:
                c.inc(S.d_st, None, 16)
            st2.st_prev2 = st2.st_prev
            st2.st_prev = S.d_st.n

        pre(0)
        pre(1)
        for g in range(NSUB):
            post(g)
            if g + 2 < NSUB:
                pre(g + 2)

    run_phase(p2)

    def p3(c):
        me, eng = c.role, c.eng

        def vstep(fn):
            if me == "vector":
                c.inc(S.e_v, fn())
                c.wait(S.e_v)
            else:
                c.inc(S.e_v, None)

        def sstep(fn):
            if me == "scalar":
                c.inc(S.e_s, fn())
                c.wait(S.e_s)
            else:
                c.inc(S.e_s, None)

        if me == "vector":
            c.wait(S.d_sum)
        vstep(lambda: eng.tensor_copy(seqrow_sb[0:1, :], ps_main[0:1, 2560:3584]))
        if me == "sync":
            c.wait(S.e_v)
            c.inc(S.e_ld, eng.dma_start(vec_scr, seqrow_sb[0:1, :]), 16)
            c.wait(S.e_ld)
            c.inc(S.e_ld, eng.dma_start(seqst_sb[:, :], AP(vec_scr.tensor, 0, [[1, 128], [128, 8]]), allow_slow_non_contiguous=True), 16)
        else:
            c.inc(S.e_ld, None, 32)
        if me == "tensor":
            c.wait(S.e_ld)
            i = None
            for k in range(8):
                i = eng.matmul(ps_main[0:1, 0:8], seqst_sb[:, k:k + 1],
                               gatew_sb[:, k * 8:(k + 1) * 8], start=(k == 0), stop=(k == 7))
            c.inc(S.e_t, i)
        else:
            c.inc(S.e_t, None)
        if me == "scalar":
            c.wait(S.e_t)
        sstep(lambda: eng.activation(prob_sb[0:1, 0:8], ps_main[0:1, 0:8], AF.Copy, scale=1.0 / T))
        if me == "vector":
            c.wait(S.e_s)
        vstep(lambda: eng.reduce_max(prob_sb[0:1, 8:9], prob_sb[0:1, 0:8], axis=AX.X, negate=True))
        if me == "scalar":
            c.wait(S.e_v)
        sstep(lambda: eng.activation(prob_sb[0:1, 16:24], prob_sb[0:1, 0:8], AF.Exp,
                                     bias=prob_sb[0:1, 8:9]))
        if me == "vector":
            c.wait(S.e_s)
        vstep(lambda: eng.reduce_sum(prob_sb[0:1, 9:10], prob_sb[0:1, 16:24], axis=AX.X))
        vstep(lambda: eng.reciprocal(prob_sb[0:1, 10:11], prob_sb[0:1, 9:10]))
        vstep(lambda: eng.tensor_scalar(prob_sb[0:1, 24:32], prob_sb[0:1, 16:24],
                                        prob_sb[0:1, 10:11], None, ALU.mult))
        vstep(lambda: eng.reduce_max(prob_sb[0:1, 11:12], prob_sb[0:1, 24:32], axis=AX.X))
        vstep(lambda: eng.tensor_scalar(prob_sb[0:1, 32:40], prob_sb[0:1, 24:32],
                                        prob_sb[0:1, 11:12], None, ALU.is_equal))
        vstep(lambda: eng.tensor_tensor(prob_sb[0:1, 40:48], prob_sb[0:1, 24:32],
                                        prob_sb[0:1, 32:40], ALU.mult))
        vstep(lambda: eng.tensor_tensor(prob_sb[0:1, 48:56], prob_sb[0:1, 24:32],
                                        prob_sb[0:1, 40:48], ALU.subtract))
        vstep(lambda: eng.reduce_max(prob_sb[0:1, 12:13], prob_sb[0:1, 48:56], axis=AX.X))
        vstep(lambda: eng.tensor_scalar(prob_sb[0:1, 56:64], prob_sb[0:1, 48:56],
                                        prob_sb[0:1, 12:13], None, ALU.is_equal))
        vstep(lambda: eng.tensor_tensor(prob_sb[0:1, 13:14], prob_sb[0:1, 11:12],
                                        prob_sb[0:1, 12:13], ALU.add))
        vstep(lambda: eng.reciprocal(prob_sb[0:1, 14:15], prob_sb[0:1, 13:14]))
        vstep(lambda: eng.tensor_tensor(prob_sb[0:1, 40:48], prob_sb[0:1, 32:40],
                                        iota8_sb[0:1, :], ALU.mult))
        vstep(lambda: eng.reduce_sum(prob_sb[0:1, 74:75], prob_sb[0:1, 40:48], axis=AX.X))
        vstep(lambda: eng.tensor_tensor(prob_sb[0:1, 40:48], prob_sb[0:1, 56:64],
                                        iota8_sb[0:1, :], ALU.mult))
        vstep(lambda: eng.reduce_sum(prob_sb[0:1, 75:76], prob_sb[0:1, 40:48], axis=AX.X))
        vstep(lambda: eng.tensor_copy(idx_sb[0:1, :], prob_sb[0:1, 74:76]))
        vstep(lambda: eng.tensor_copy(prob_sb[0:1, 72:73], prob_sb[0:1, 11:12]))
        vstep(lambda: eng.tensor_copy(prob_sb[0:1, 73:74], prob_sb[0:1, 12:13]))
        vstep(lambda: eng.tensor_scalar(prob_sb[0:1, 72:74], prob_sb[0:1, 72:74],
                                        prob_sb[0:1, 14:15], None, ALU.mult))
        vstep(lambda: eng.tensor_scalar(prob_sb[0:1, 32:40], prob_sb[0:1, 32:40],
                                        prob_sb[0:1, 11:12], None, ALU.mult))
        vstep(lambda: eng.tensor_scalar(prob_sb[0:1, 56:64], prob_sb[0:1, 56:64],
                                        prob_sb[0:1, 12:13], None, ALU.mult))
        vstep(lambda: eng.tensor_tensor(prob_sb[0:1, 64:72], prob_sb[0:1, 32:40],
                                        prob_sb[0:1, 56:64], ALU.add))
        vstep(lambda: eng.tensor_scalar(prob_sb[0:1, 64:72], prob_sb[0:1, 64:72],
                                        prob_sb[0:1, 14:15], None, ALU.mult))
        if me == "sync":
            c.wait(S.e_v)
            c.inc(S.e_ld, eng.dma_start(AP(vec_scr.tensor, 0, [[NE, 1], [1, NE]]),
                                        prob_sb[0:1, 64:72]), 16)
            c.wait(S.e_ld)
            c.inc(S.e_ld, eng.dma_start(ccol_sb[:, :], AP(vec_scr.tensor, 0, [[1, 8], [1, 1]]), allow_slow_non_contiguous=True), 16)
        else:
            c.inc(S.e_ld, None, 32)
        if me == "tensor":
            c.wait(S.e_ld)
            i = None
            for h in range(2):
                i = eng.matmul(ps_main[0:1, 1024 + h * 512:1024 + (h + 1) * 512],
                               ccol_sb[:, 0:1], b2_sb[:, h * 512:(h + 1) * 512],
                               start=True, stop=True)
            c.inc(S.e_t, i)
        else:
            c.inc(S.e_t, None)
        if me == "vector":
            c.wait(S.e_t)
        vstep(lambda: eng.tensor_copy(cb2row_sb[0:1, :], ps_main[0:1, 1024:2048]))
        if me == "tensor":
            c.wait(S.e_v)
            for h in range(2):
                eng.matmul(ps_main[:, h * 512:(h + 1) * 512], ones_sb[0:1, 0:128],
                           cb2row_sb[0:1, h * 512:(h + 1) * 512], start=True, stop=True)
            i = eng.matmul(ps_main[:, 2048:2058], ones_sb[0:1, 0:128],
                           prob_sb[0:1, 64:74], start=True, stop=True)
            c.inc(S.e_t, i)
        else:
            c.inc(S.e_t, None)
        if me == "vector":
            c.wait(S.e_t)
        vstep(lambda: eng.tensor_copy(cb2_repl[:, :], ps_main[:, 0:1024]))
        vstep(lambda: eng.tensor_copy(c_repl[:, :], ps_main[:, 2048:2058]))
        if dbg_d is not None:
            if me == "sync":
                c.wait(S.e_v)
                c.inc(S.e_ld, eng.dma_start(dbg_d, prob_sb[0:1, :]), 16)
                c.wait(S.e_ld)
            else:
                c.inc(S.e_ld, None, 16)

    run_phase(p3)
    p23.close()

    ffn = ExitStack()
    w1_sb = sb(ffn, "w1_sb", [128, 8 * FF], BF16)
    w2_sb = sb(ffn, "w2_sb", [128, 32 * E], BF16)
    h1t_sb = sb(ffn, "h1t_sb", [128, 32 * TCH], BF16)
    xn1tc_sb = sb(ffn, "xn1tc_sb", [128, 2 * 8 * TCH], BF16)
    resid_sb = sb(ffn, "resid_sb", [128, 4 * E], F32)
    outb_sb = sb(ffn, "outb_sb", [128, 4 * E], F32)
    b1t_sb = sb(ffn, "b1t_sb", [128, 32], F32)

    NSLOT = 2

    def p4(c):
        from concourse.expressions import RuntimeValue
        me, eng = c.role, c.eng
        h1done_prev = 0
        ydone_prev = 0
        geldone_prev = 0
        xnc_free = [0, 0]
        res_free = [0, 0]
        outb_free = [0, 0]
        gel_free = [0, 0]
        yps_free = [0, 0]
        ych_prev = 0
        if me == "sync":
            c.wait(S.d_st)
        for e in range(NSLOT):
            src_scr = x1_scr if e == 0 else part_scr[(e - 1) % 2]
            dst_scr = out_d if e == NSLOT - 1 else part_scr[e % 2]
            if me == "gpsimd":
                regs = eng.alloc_register(f"p4w{e}")
                regb = eng.alloc_register(f"p4b{e}")
                eng.reg_load(regs, idx_sb[0:1, e:e + 1])
                eng.reg_mov(regb, regs)
                eng.reg_alu(regs, regs, E * FF, ALU.mult)
                eng.reg_alu(regb, regb, FF, ALU.mult)
                offw = RuntimeValue(regs, 0, (NE - 1) * E * FF)
                offb = RuntimeValue(regb, 0, (NE - 1) * FF)
                c.wait(S.f_h1, h1done_prev)
                HF = FF // 2
                w1ap = w1_sb[:, :]
                for hf in range(2):
                    src = AP(w1_d.tensor, offw + hf * HF, [[FF, 128], [128 * FF, 8], [1, HF]])
                    dst = AP(w1ap.tensor, w1ap.offset + hf * HF,
                             [[w1ap.ap[0][0], 128], [FF, 8], [1, HF]])
                    c.inc(S.f_w, eng.dma_start(dst, src), 16)
                c.wait(S.f_gel, geldone_prev)
                src = AP(b1_d.tensor, offb, [[1, 128], [128, 32]])
                c.inc(S.f_w, eng.dma_start(b1t_sb[:, :], src, allow_slow_non_contiguous=True), 16)
                c.wait(S.f_y, ydone_prev)
                for hf in range(2):
                    src = AP(w2_d.tensor, offw + hf * 16 * 128 * E, [[E, 128], [128 * E, 16], [1, E]])
                    c.inc(S.f_w2, eng.dma_start(w2_sb[:, hf * 16 * E:(hf + 1) * 16 * E], src), 16)
            else:
                c.inc(S.f_w, None, 48)
                c.inc(S.f_w2, None, 32)
            for ch in range(NCH):
                cbuf = ch % 2
                if me == "sync":
                    c.wait(S.f_h1, xnc_free[cbuf])
                    src = AP(xn1t_scr.tensor, ch * TCH, [[T, 128], [128 * T, 8], [1, TCH]])
                    c.inc(S.f_xn, eng.dma_start(xn1tc_sb[:, cbuf * 8 * TCH:(cbuf + 1) * 8 * TCH], src), 16)
                    c.wait(S.f_ev, res_free[cbuf])
                    if e > 0:
                        c.wait(S.f_st, 16 * ((e - 1) * NCH + ch + 1))
                    src = AP(src_scr.tensor, ch * TCH * E, [[E, 128], [128 * E, 2], [1, E]])
                    c.inc(S.f_res, eng.dma_start(resid_sb[:, cbuf * 2 * E:(cbuf + 1) * 2 * E], src), 16)
                else:
                    c.inc(S.f_xn, None, 16)
                    c.inc(S.f_res, None, 16)
                for m in range(32):
                    if me == "tensor":
                        if m == 0:
                            c.wait(S.f_w, 16 * (3 * e + 1))
                            c.wait(S.f_xn, 16 * (e * NCH + ch + 1))
                        if m == 16:
                            c.wait(S.f_w, 16 * (3 * e + 2))
                        c.wait(S.f_gel, gel_free[m % 2])
                        hoff = (m % 2) * 512
                        i = None
                        for k in range(8):
                            i = eng.matmul(ps_main[:, hoff:hoff + TCH],
                                           w1_sb[:, k * FF + m * 128:k * FF + (m + 1) * 128],
                                           xn1tc_sb[:, cbuf * 8 * TCH + k * TCH:cbuf * 8 * TCH + (k + 1) * TCH],
                                           start=(k == 0), stop=(k == 7))
                        c.inc(S.f_h1, i)
                    else:
                        c.inc(S.f_h1, None)
                    if me == "scalar":
                        c.wait(S.f_h1)
                        if m == 0:
                            c.wait(S.f_w, 16 * (3 * e + 3))
                            c.wait(S.f_y, ych_prev)
                        hoff = (m % 2) * 512
                        i = eng.activation(h1t_sb[:, m * TCH:(m + 1) * TCH],
                                           ps_main[:, hoff:hoff + TCH], AF.Gelu,
                                           bias=b1t_sb[:, m:m + 1])
                        c.inc(S.f_gel, i)
                    else:
                        c.inc(S.f_gel, None)
                    gel_free[m % 2] = S.f_gel.n
                xnc_free[cbuf] = S.f_h1.n
                if ch == NCH - 1:
                    h1done_prev = S.f_h1.n
                    geldone_prev = S.f_gel.n
                for t in range(2):
                    yoff = 1024 + t * 1024
                    if me == "tensor":
                        c.wait(S.f_gel)
                        c.wait(S.f_ev, yps_free[t])
                        i = None
                        for k2 in range(32):
                            if ch == 0 and t == 0 and k2 % 16 == 0:
                                c.wait(S.f_w2, 16 * (2 * e + k2 // 16 + 1))
                            for h in range(2):
                                i = eng.matmul(ps_main[:, yoff + h * 512:yoff + (h + 1) * 512],
                                               h1t_sb[:, k2 * TCH + t * 128:k2 * TCH + t * 128 + 128],
                                               w2_sb[:, k2 * E + h * 512:k2 * E + (h + 1) * 512],
                                               start=(k2 == 0), stop=(k2 == 31))
                        c.inc(S.f_y, i)
                    else:
                        c.inc(S.f_y, None)
                    if me == "vector":
                        c.wait(S.f_y)
                        c.wait(S.f_res)
                        c.wait(S.f_st, outb_free[cbuf])
                        ob = outb_sb[:, cbuf * 2 * E + t * E:cbuf * 2 * E + (t + 1) * E]
                        i = eng.scalar_tensor_tensor(ob,
                                                     ps_main[:, yoff:yoff + 1024],
                                                     c_repl[:, 8 + e:9 + e],
                                                     resid_sb[:, cbuf * 2 * E + t * E:cbuf * 2 * E + (t + 1) * E],
                                                     ALU.mult, ALU.add)
                        if e == 0:
                            i = eng.tensor_tensor(ob, ob, cb2_repl[:, :], ALU.add)
                        c.inc(S.f_ev, i)
                    else:
                        c.inc(S.f_ev, None)
                    yps_free[t] = S.f_ev.n
                if ch == NCH - 1:
                    ydone_prev = S.f_y.n
                ych_prev = S.f_y.n
                res_free[cbuf] = S.f_ev.n
                if me == "gpsimd":
                    c.wait(S.f_ev)
                    dst = AP(dst_scr.tensor, ch * TCH * E, [[E, 128], [128 * E, 2], [1, E]])
                    c.inc(S.f_st, eng.dma_start(dst, outb_sb[:, cbuf * 2 * E:(cbuf + 1) * 2 * E]), 16)
                else:
                    c.inc(S.f_st, None, 16)
                outb_free[cbuf] = S.f_st.n
        if me == "sync":
            c.wait(S.f_st)

    run_phase(p4)
    ffn.close()
    mid.close()
    glob.close()
    ctxstack.close()
    return nc


_NC_CACHE = None


def kernel(L0_attn_x, L0_attn_W_kqv, L0_attn_W_proj, L0_attn_b_proj,
           L0_ffn_gate_w, L0_ffn_w1, L0_ffn_b1, L0_ffn_w2, L0_ffn_b2):
    global _NC_CACHE
    if _NC_CACHE is None:
        _NC_CACHE = build()
    nc = _NC_CACHE
    x = np.ascontiguousarray(np.asarray(L0_attn_x, dtype=np.float32))
    common = {
        "wkqv": np.ascontiguousarray(np.asarray(L0_attn_W_kqv, dtype=np.float32)),
        "wproj": np.ascontiguousarray(np.asarray(L0_attn_W_proj, dtype=np.float32)),
        "bproj": np.ascontiguousarray(np.asarray(L0_attn_b_proj, dtype=np.float32)),
        "gatew": np.ascontiguousarray(np.asarray(L0_ffn_gate_w, dtype=np.float32)),
        "w1": np.ascontiguousarray(np.asarray(L0_ffn_w1, dtype=np.float32)),
        "b1": np.ascontiguousarray(np.asarray(L0_ffn_b1, dtype=np.float32)),
        "w2": np.ascontiguousarray(np.asarray(L0_ffn_w2, dtype=np.float32)),
        "b2": np.ascontiguousarray(np.asarray(L0_ffn_b2, dtype=np.float32)),
    }
    in_maps = [dict(common, x=np.ascontiguousarray(x[b])) for b in range(8)]
    res = run_bass_kernel_spmd(nc, in_maps, core_ids=list(range(8)))
    return np.stack([res.results[b]["out"] for b in range(8)], axis=0)
